# version 45 (final)
# baseline (speedup 1.0000x reference)
import contextlib
import numpy as np
import concourse.bass as bass
import concourse.mybir as mybir
from concourse.bass_utils import run_bass_kernel_spmd
import ml_dtypes

F32 = mybir.dt.float32
BF16 = mybir.dt.bfloat16
FP8 = mybir.dt.float8e5
AF = mybir.ActivationFunctionType
ALU = mybir.AluOpType

C = 1024
CT = 8
DFF = 4096
HT = 32
NCORES = 8
TLOC = 2048
HALO = 48
TTOT = TLOC + HALO
BLOCKS = [(0, 48), (48, 512), (560, 512), (1072, 512), (1584, 512)]

ENGS = ("sync", "scalar", "vector", "tensor", "gpsimd")
SERIAL = False


class Plan:
    def __init__(self, nc, stack):
        self.nc = nc
        self.q = {e: [] for e in ENGS}
        self.sem = {}
        self.cnt = {}
        for name in ("lds0", "lds1", "sts0", "sts1", "ldg0", "ldg1",
                     "wc0", "wc1", "t", "v", "s", "g"):
            self.sem[name] = stack.enter_context(nc.semaphore(name))
            self.cnt[name] = 0
        self.eng_sem = {"sync": "lds0", "scalar": "s", "vector": "v",
                        "tensor": "t", "gpsimd": "g"}
        self.sem_owner = {"lds0": "sync", "lds1": "sync", "sts0": "scalar",
                          "sts1": "scalar", "ldg0": "gpsimd", "ldg1": "gpsimd",
                          "wc0": "gpsimd", "wc1": "gpsimd",
                          "t": "tensor", "v": "vector", "s": "scalar",
                          "g": "gpsimd"}
        self.dma_rr = {"lds": 0, "sts": 0, "ldg": 0, "wc": 0}
        self.seen = {e: {} for e in ENGS}

    def dma(self, eng, fn, pool, rbufs_toks=()):
        i = self.dma_rr[pool]
        self.dma_rr[pool] = 1 - i
        sname = f"{pool}{i}"
        waits = list(rbufs_toks) + [(sname, self.cnt[sname])]
        return self.op(eng, fn, waits, sname=sname, inc=16)

    def raw_wait(self, eng, sname, val):
        if val <= 0 or self.seen[eng].get(sname, 0) >= val:
            return
        self.seen[eng][sname] = val
        h = self.sem[sname]
        self.q[eng].append(lambda e, h=h, val=val: e.wait_ge(h, val))

    def op(self, eng, fn, waits=(), sname=None, inc=1):
        if SERIAL:
            waits = list(waits) + [(sn, self.cnt[sn]) for sn in
                                   ("lds0", "lds1", "sts0", "sts1", "ldg0",
                                    "ldg1", "t", "v", "s")]
        best = {}
        for sn, v in waits:
            if v and v > best.get(sn, 0):
                best[sn] = v
        for sn, v in best.items():
            self.raw_wait(eng, sn, v)
        if sname is None:
            sname = self.eng_sem[eng]
        self.cnt[sname] += inc
        val = self.cnt[sname]
        h = self.sem[sname]

        def build(e, fn=fn, h=h, inc=inc):
            r = fn(e)
            if inc:
                r.then_inc(h, inc)
        self.q[eng].append(build)
        return (sname, val)


class Buf:
    def __init__(self, name):
        self.name = name
        self.writers = []
        self.readers = []

    def wdeps(self):
        return list(self.readers) + list(self.writers)

    def rdeps(self):
        return list(self.writers)

    def setw(self, tok):
        self.writers = list(tok) if isinstance(tok, list) else [tok]
        self.readers = []

    def addr(self, tok):
        self.readers.append(tok)


def _threefry2x32(k0, k1, x0, x1):
    rot = [13, 15, 26, 6, 17, 29, 16, 24]
    x0 = x0.astype(np.uint32).copy()
    x1 = x1.astype(np.uint32).copy()
    ks = [np.uint32(k0), np.uint32(k1),
          np.uint32(np.uint32(k0) ^ np.uint32(k1) ^ np.uint32(0x1BD11BDA))]
    x0 += ks[0]
    x1 += ks[1]
    for i in range(5):
        for r in rot[4 * (i % 2):4 * (i % 2) + 4]:
            x0 += x1
            x1 = (x1 << np.uint32(r)) | (x1 >> np.uint32(32 - r))
            x1 ^= x0
        x0 += ks[(i + 1) % 3]
        x1 += ks[(i + 2) % 3] + np.uint32(i + 1)
    return x0, x1


def _mk_mask625():
    n = 16384 * 1024
    i64 = np.arange(n, dtype=np.uint64)
    h0, h1 = _threefry2x32(0, 42, (i64 >> np.uint64(32)).astype(np.uint32),
                           i64.astype(np.uint32))
    import os
    if os.environ.get("BASS_SIM_FP64"):
        b64 = (h0.astype(np.uint64) << np.uint64(32)) | h1.astype(np.uint64)
        fl64 = ((b64 >> np.uint64(12)) | np.uint64(0x3FF0000000000000)) \
            .view(np.float64) - 1.0
        keep = (fl64 < 0.8).reshape(16384, 1024)
    else:
        bits = h0 ^ h1
        fl = ((bits >> np.uint32(9)) | np.uint32(0x3F800000)).view(np.float32) \
            - np.float32(1.0)
        keep = (fl < np.float32(0.8)).reshape(16384, 1024)
    return np.where(keep, np.float32(0.625), np.float32(0.0)).astype(np.float32)


def _tile_cols(arrT, s, n, dtype):
    blk = arrT[:, s:s + n]
    nt = blk.shape[0] // 128
    out = blk.reshape(nt, 128, n).transpose(1, 0, 2).reshape(128, nt * n)
    return np.ascontiguousarray(out).astype(dtype)


def _prep_weightT(w):
    wT = np.ascontiguousarray(np.asarray(w, np.float32).T)
    nt = wT.shape[0] // 128
    return np.ascontiguousarray(
        wT.reshape(nt, 128, wT.shape[1]).transpose(1, 0, 2).reshape(128, nt * wT.shape[1]))


def _colN(vec, nt):
    return np.ascontiguousarray(np.asarray(vec, np.float32).reshape(nt, 128).T)


def build_program(debug=False):
    nc = bass.Bass("TRN2", target_bir_lowering=False, debug=False)

    d_in = {}
    for b, (s, n) in enumerate(BLOCKS):
        d_in[f"x0_{b}"] = nc.dram_tensor(f"x0_{b}", [128, CT * n], F32, kind="ExternalInput").ap()
        d_in[f"mk_{b}"] = nc.dram_tensor(f"mk_{b}", [128, CT * n], BF16, kind="ExternalInput").ap()
    d_in["w1t"] = nc.dram_tensor("w1t", [128, CT * DFF], F32, kind="ExternalInput").ap()
    d_in["w2t"] = nc.dram_tensor("w2t", [128, HT * C], F32, kind="ExternalInput").ap()
    for nm in ("wkt", "wvt", "wrt", "wot"):
        d_in[nm] = nc.dram_tensor(nm, [128, CT * C], F32, kind="ExternalInput").ap()
    d_in["cst"] = nc.dram_tensor("cst", [128, HT + 7 * CT], F32,
                                 kind="ExternalInput").ap()

    dbg = {}
    if debug:
        dbg["uq"] = nc.dram_tensor("dbg_uq", [128, CT * 128], FP8, kind="ExternalOutput").ap()
        dbg["hq"] = nc.dram_tensor("dbg_hq", [128, HT * 128], FP8, kind="ExternalOutput").ap()
        dbg["yb"] = nc.dram_tensor("dbg_yb", [128, CT * 128], BF16, kind="ExternalOutput").ap()
        dbg["xb"] = nc.dram_tensor("dbg_xb", [128, CT * 128], BF16, kind="ExternalOutput").ap()
        dbg["k1"] = nc.dram_tensor("dbg_k1", [128, CT * 512], F32, kind="ExternalOutput").ap()
        dbg["v1"] = nc.dram_tensor("dbg_v1", [128, CT * 512], F32, kind="ExternalOutput").ap()
        dbg["sr1"] = nc.dram_tensor("dbg_sr1", [128, CT * 512], BF16, kind="ExternalOutput").ap()
        dbg["srw1"] = nc.dram_tensor("dbg_srw1", [128, CT * 512], BF16, kind="ExternalOutput").ap()
        dbg["pab"] = nc.dram_tensor("dbg_pab", [128, 4 * 513], F32, kind="ExternalOutput").ap()
    spill_kind = "ExternalOutput" if debug else "Internal"
    x1sp, x2sp, outs = {}, {}, {}
    for b, (s, n) in enumerate(BLOCKS):
        x1sp[b] = nc.dram_tensor(f"x1sp_{b}", [128, CT * n], F32, kind=spill_kind).ap()
        if b > 0:
            x2sp[b] = nc.dram_tensor(f"x2sp_{b}", [128, CT * n], F32, kind=spill_kind).ap()
            outs[b] = nc.dram_tensor(f"x3_{b}", [128, CT * n], F32, kind="ExternalOutput").ap()

    with contextlib.ExitStack() as ST:
        def sb(name, shape, dt):
            return ST.enter_context(nc.sbuf_tensor(name, shape, dt))

        A_w1 = sb("A_w1", [128, CT * DFF], FP8)
        A_w2 = sb("A_w2", [128, HT * C], FP8)
        A_xin = sb("A_xin", [128, 2 * CT * 512], F32)
        A_kv = sb("A_kv", [128, 2 * CT * 512], F32)
        A_hq = sb("A_hq", [128, HT * 512], FP8)
        A_wk = sb("A_wk", [128, 4 * 512], F32)
        WKB = [A_wk[:, i * 512:(i + 1) * 512] for i in range(4)]
        A_msk = sb("A_msk", [128, CT * 512], BF16)
        A_yb = sb("A_yb", [128, CT * 512], BF16)
        A_d = sb("A_d", [128, CT * 512], BF16)
        A_xsq = sb("A_xsq", [128, CT * 512], BF16)
        A_srw = sb("A_srw", [128, CT * 512], BF16)
        A_u2 = sb("A_u2", [128, CT * 513], BF16)

        c_all = sb("c_all", [128, HT + 7 * CT], F32)
        c_b1 = c_all[:, 0:HT]
        c_b2 = c_all[:, HT:HT + CT]
        c_b1q8 = sb("c_b1q8", [128, HT], FP8)
        c_b2q8 = sb("c_b2q8", [128, CT], FP8)
        c_td = c_all[:, HT + CT:HT + 2 * CT]
        c_u = c_all[:, HT + 2 * CT:HT + 3 * CT]
        c_tmk = c_all[:, HT + 3 * CT:HT + 4 * CT]
        c_tmv = c_all[:, HT + 4 * CT:HT + 5 * CT]
        c_tmr = c_all[:, HT + 5 * CT:HT + 6 * CT]
        c_kb = c_all[:, HT + 6 * CT:HT + 7 * CT]
        c_w = sb("c_w", [128, CT], F32)
        st_all = sb("st_all", [128, 3 * CT], F32)
        A_pab2 = sb("A_pab2", [128, 1539], F32)
        ones_bf = sb("ones_bf", [128, 1], BF16)
        ones_k1 = sb("ones_k1", [1, 128], F32)

        ps_s1 = ST.enter_context(nc.psum_tensor("ps_s1", [1, 512], F32))
        ps_s2 = ST.enter_context(nc.psum_tensor("ps_s2", [1, 512], F32))
        ps_bc1 = ST.enter_context(nc.psum_tensor("ps_bc1", [128, 512], F32))
        ps_bc2 = ST.enter_context(nc.psum_tensor("ps_bc2", [128, 512], F32))
        ps_mm = [ST.enter_context(nc.psum_tensor(f"ps_mm{i}", [128, 512], F32))
                 for i in range(4)]

        P = Plan(nc, ST)
        B = {}

        def buf(name):
            if name not in B:
                B[name] = Buf(name)
            return B[name]

        def w_op(eng, fn, wbufs, rbufs, sname=None, inc=1):
            waits = []
            for bu in rbufs:
                waits += bu.rdeps()
            for bu in wbufs:
                waits += bu.wdeps()
            if inc == 16:
                tok = P.dma(eng, fn, {"sync": "lds", "scalar": "sts",
                                      "gpsimd": "ldg"}[eng], waits)
            else:
                tok = P.op(eng, fn, waits, sname=sname, inc=inc)
            for bu in rbufs:
                bu.addr(tok)
            for bu in wbufs:
                bu.setw(tok)
            return tok

        def link(dst, src):
            dst.readers += src.writers + src.readers

        u2v = A_u2[:].rearrange("p (c n) -> p c n", c=CT)
        kv8 = A_kv[:].bitcast(FP8)
        stage_ab = A_kv[:, 0:CT * 512]
        uq_slots = [kv8[:, 16384 + p * 4096: 16384 + (p + 1) * 4096]
                    for p in range(2)]
        ab_tmp = A_kv[:, 6144:8192]
        kbv = A_kv[:, 0:CT * 512]
        vbv = A_kv[:, CT * 512:2 * CT * 512]
        stage_cde = A_hq[:].bitcast(F32)
        mskf = A_msk[:].bitcast(F32)
        A_p = mskf[:, 0:513]
        A_a = mskf[:, 513:1026]
        A_b = mskf[:, 1026:1539]

        def prologue_memsets():
            w_op("vector", lambda e: e.memset(ones_bf[:], 1.0), [buf("ones_bf")], [])
            w_op("vector", lambda e: e.memset(ones_k1[:], 1.0), [buf("ones_k1")], [])
            w_op("vector", lambda e: e.memset(st_all[:, 0:CT], -1e38), [buf("st_all")], [])
            w_op("vector", lambda e: e.memset(st_all[:, CT:3 * CT], 0.0), [buf("st_all")], [])

        def prologue_consts():
            tok = P.dma("sync", lambda e: e.dma_start(out=c_all[:], in_=d_in["cst"]),
                        "lds", [])
            for nm in ("b1c", "b2c", "tdc", "tfc", "tmkc", "tmvc", "tmrc", "kbc"):
                buf(nm).setw(tok)
            w_op("scalar", lambda e: e.activation(out=c_w[:], in_=c_td, func=AF.Exp),
                 [buf("c_w")], [buf("tdc")])
            w_op("vector", lambda e: e.tensor_scalar(out=c_w[:], in0=c_w[:], scalar1=-1.0,
                                                     scalar2=None, op0=ALU.mult),
                 [buf("c_w")], [buf("c_w")])
            w_op("scalar", lambda e: e.copy(out=c_b1q8[:], in_=c_b1),
                 [buf("b1q8")], [buf("b1c")])
            w_op("scalar", lambda e: e.copy(out=c_b1, in_=c_b1q8[:]),
                 [buf("b1c")], [buf("b1q8")])
            w_op("scalar", lambda e: e.copy(out=c_b2q8[:], in_=c_b2),
                 [buf("b2q8")], [buf("b2c")])
            w_op("scalar", lambda e: e.copy(out=c_b2, in_=c_b2q8[:]),
                 [buf("b2c")], [buf("b2q8")])

        def cast_ffn_weights():
            for dst, srcn, bufs in (
                    (A_w1, "w1t", [buf("A_w1"), buf("A_w1b")]),
                    (A_w2, "w2t", [buf("A_w2"), buf("A_w2b")])):
                X = CT * DFF
                waits = []
                for bu in bufs:
                    waits += bu.wdeps()
                toks = []
                for h in range(2):
                    lo, hi = h * (X // 2), (h + 1) * (X // 2)
                    toks.append(P.dma(
                        "gpsimd",
                        lambda e, lo=lo, hi=hi, dst=dst, srcn=srcn: e.dma_start(
                            out=dst[:, lo:hi], in_=d_in[srcn][:, lo:hi]),
                        "ldg", waits))
                for bu in bufs:
                    bu.setw(list(toks))

        def _cast_dma(dst_ap, srcn, bu, pool):
            waits = bu.wdeps()
            tok = P.dma("gpsimd",
                        lambda e, dst_ap=dst_ap, srcn=srcn: e.dma_start(
                            out=dst_ap, in_=d_in[srcn]),
                        pool, waits)
            bu.setw(tok)

        def cast_attn_weights_kvr():
            v1 = A_w1[:].bitcast(BF16)
            v2 = A_w2[:].bitcast(BF16)
            _cast_dma(v1[:, 0:CT * C], "wkt", buf("A_w1"), "wc")
            _cast_dma(v1[:, CT * C:2 * CT * C], "wvt", buf("A_w1b"), "wc")
            _cast_dma(v2[:, 0:CT * C], "wrt", buf("A_w2"), "wc")
            return v1, v2

        def cast_attn_weights_o():
            v2 = A_w2[:].bitcast(BF16)
            _cast_dma(v2[:, CT * C:2 * CT * C], "wot", buf("A_w2b"), "wc")

        def plan_ln(xa_flat, xa3, xbf, n, xb, xb_buf, dest_halves, dest_buf,
                    tmp_halves, tmp_buf, skip_cast=False):
            if not skip_cast:
                w_op("vector", lambda e: e.tensor_copy(out=xb, in_=xa_flat),
                     [xb_buf], [xbf])
            xsq = A_xsq[:, 0:CT * n]
            w_op("vector", lambda e, xsq=xsq: e.tensor_tensor(out=xsq, in0=xb, in1=xb,
                                                              op=ALU.mult),
                 [buf("A_xsq")], [xb_buf])
            def stats_group(psd, psb, srcbuf, src):
                waits = list(srcbuf.rdeps()) + list(buf("ones_bf").rdeps()) + \
                    list(srcbuf.writers) + psb.wdeps()
                tok = None
                for ct in range(CT):
                    fn = (lambda e, ct=ct: e.matmul(
                        psd[0:1, 0:n], ones_bf[:], src[:, ct * n:(ct + 1) * n],
                        start=(ct == 0), stop=(ct == CT - 1)))
                    if ct == 0:
                        P.op("tensor", fn, waits, inc=0)
                    elif ct == CT - 1:
                        tok = P.op("tensor", fn, (), inc=1)
                    else:
                        P.op("tensor", fn, (), inc=0)
                srcbuf.addr(tok)
                buf("ones_bf").addr(tok)
                psb.setw(tok)
            stats_group(ps_s1, buf("ps_s1"), xb_buf, xb)
            stats_group(ps_s2, buf("ps_s2"), buf("A_xsq"), xsq)

            mrow = WKB[0][0:1, 0:n]
            erow = WKB[1][0:1, 0:n]
            m2 = WKB[2][0:1, 0:n]
            w_op("vector", lambda e: e.tensor_scalar(out=mrow, in0=ps_s1[0:1, 0:n],
                                                     scalar1=float(1.0 / C),
                                                     scalar2=None, op0=ALU.mult),
                 [buf("wk0")], [buf("ps_s1")])
            w_op("vector", lambda e: e.tensor_scalar(out=erow, in0=ps_s2[0:1, 0:n],
                                                     scalar1=float(1.0 / C),
                                                     scalar2=None, op0=ALU.mult),
                 [buf("wk1")], [buf("ps_s2")])
            w_op("vector", lambda e: e.tensor_tensor(out=m2, in0=mrow, in1=mrow,
                                                     op=ALU.mult),
                 [buf("wk2")], [buf("wk0")])
            w_op("vector", lambda e: e.tensor_tensor(out=erow, in0=erow, in1=m2,
                                                     op=ALU.subtract),
                 [buf("wk1")], [buf("wk1"), buf("wk2")])
            w_op("vector", lambda e: e.tensor_scalar(out=erow, in0=erow,
                                                     scalar1=1e-5, scalar2=None,
                                                     op0=ALU.add),
                 [buf("wk1")], [buf("wk1")])
            w_op("scalar", lambda e: e.activation(out=erow, in_=erow, func=AF.Ln),
                 [buf("wk1")], [buf("wk1")])
            w_op("scalar", lambda e: e.activation(out=erow, in_=erow, func=AF.Exp,
                                                  scale=-0.5),
                 [buf("wk1")], [buf("wk1")])
            w_op("tensor", lambda e: e.matmul(ps_bc1[:, 0:n], ones_k1[:], mrow,
                                              start=True, stop=True),
                 [buf("ps_bc1")], [buf("wk0"), buf("ones_k1")])
            w_op("tensor", lambda e: e.matmul(ps_bc2[:, 0:n], ones_k1[:], erow,
                                              start=True, stop=True),
                 [buf("ps_bc2")], [buf("wk1"), buf("ones_k1")])
            h = CT // 2
            for g in range(2):
                xs = xa3[:, g * h:(g + 1) * h, :]
                bc1 = ps_bc1[:, 0:n].unsqueeze(1).broadcast_to([128, h, n])
                bc2 = ps_bc2[:, 0:n].unsqueeze(1).broadcast_to([128, h, n])
                w_op("vector", lambda e, xs=xs, g=g, bc1=bc1: e.tensor_tensor(
                        out=tmp_halves[g], in0=xs, in1=bc1, op=ALU.subtract),
                     [tmp_buf], [xbf, buf("ps_bc1")])
                w_op("vector", lambda e, g=g, bc2=bc2: e.tensor_tensor(
                        out=dest_halves[g], in0=tmp_halves[g], in1=bc2,
                        op=ALU.mult),
                     [dest_buf], [tmp_buf, buf("ps_bc2")])

        def plan_ffn(n, ybv, uq_flat, uqb):
            hq = A_hq[:]
            DR = mybir.MatmulPerfMode.DoubleRow
            dr = n >= 256
            w13 = A_w1[:].rearrange("p (c d) -> p c d", c=CT)
            uq3 = uq_flat[:, 0:CT * n].rearrange("p (c n) -> p c n", c=CT)
            for j in range(HT):
                psd = ps_mm[j % 2]
                psb = buf(f"ps_mm{j % 2}")
                waits = uqb.rdeps() + buf("A_w1").rdeps() + \
                    buf("A_w1b").rdeps() + psb.wdeps()
                tok = None
                if dr:
                    for t in range(CT // 2):
                        fn = (lambda e, t=t, j=j, psd=psd: e.matmul(
                            psd[:, 0:n],
                            w13[:, 2 * t:2 * t + 2, j * 128:(j + 1) * 128],
                            uq3[:, 2 * t:2 * t + 2, :],
                            start=(t == 0), stop=(t == CT // 2 - 1),
                            perf_mode=DR))
                        if t == 0:
                            P.op("tensor", fn, waits, inc=0)
                        elif t == CT // 2 - 1:
                            tok = P.op("tensor", fn, (), inc=1)
                        else:
                            P.op("tensor", fn, (), inc=0)
                else:
                    for ct in range(CT):
                        fn = (lambda e, ct=ct, j=j, psd=psd: e.matmul(
                            psd[:, 0:n],
                            A_w1[:, ct * DFF + j * 128: ct * DFF + (j + 1) * 128],
                            uq_flat[:, ct * n:(ct + 1) * n],
                            start=(ct == 0), stop=(ct == CT - 1)))
                        if ct == 0:
                            P.op("tensor", fn, waits, inc=0)
                        elif ct == CT - 1:
                            tok = P.op("tensor", fn, (), inc=1)
                        else:
                            P.op("tensor", fn, (), inc=0)
                uqb.addr(tok)
                buf("A_w1").addr(tok)
                buf("A_w1b").addr(tok)
                psb.setw(tok)
                w_op("scalar", lambda e, j=j, psd=psd: e.activation(
                        out=hq[:, j * n:(j + 1) * n], in_=psd[:, 0:n], func=AF.Relu,
                        bias=c_b1[:, j:j + 1], scale=1.0),
                     [buf("A_hq")], [psb, buf("b1c")])
            w23 = A_w2[:].rearrange("p (c d) -> p c d", c=HT)
            hq3 = hq[:, 0:HT * n].rearrange("p (c n) -> p c n", c=HT)
            for j2 in range(CT):
                psd = ps_mm[2 + j2 % 2]
                psb = buf(f"ps_mm{2 + j2 % 2}")
                waits = buf("A_hq").rdeps() + buf("A_w2").rdeps() + \
                    buf("A_w2b").rdeps() + psb.wdeps()
                tok = None
                if dr:
                    for t in range(HT // 2):
                        fn = (lambda e, t=t, j2=j2, psd=psd: e.matmul(
                            psd[:, 0:n],
                            w23[:, 2 * t:2 * t + 2, j2 * 128:(j2 + 1) * 128],
                            hq3[:, 2 * t:2 * t + 2, :],
                            start=(t == 0), stop=(t == HT // 2 - 1),
                            perf_mode=DR))
                        if t == 0:
                            P.op("tensor", fn, waits, inc=0)
                        elif t == HT // 2 - 1:
                            tok = P.op("tensor", fn, (), inc=1)
                        else:
                            P.op("tensor", fn, (), inc=0)
                else:
                    for ht in range(HT):
                        fn = (lambda e, ht=ht, j2=j2, psd=psd: e.matmul(
                            psd[:, 0:n],
                            A_w2[:, ht * C + j2 * 128: ht * C + (j2 + 1) * 128],
                            hq[:, ht * n:(ht + 1) * n],
                            start=(ht == 0), stop=(ht == HT - 1)))
                        if ht == 0:
                            P.op("tensor", fn, waits, inc=0)
                        elif ht == HT - 1:
                            tok = P.op("tensor", fn, (), inc=1)
                        else:
                            P.op("tensor", fn, (), inc=0)
                buf("A_hq").addr(tok)
                buf("A_w2").addr(tok)
                buf("A_w2b").addr(tok)
                psb.setw(tok)
                w_op("scalar", lambda e, j2=j2, psd=psd: e.activation(
                        out=ybv[:, j2 * n:(j2 + 1) * n], in_=psd[:, 0:n],
                        func=AF.Identity, bias=c_b2[:, j2:j2 + 1], scale=1.0),
                     [buf("A_yb")], [psb, buf("b2c")])

        def plan_ffn_phase(srcsp, dstsp, blocks, poff=0, dr_prefix=None,
                           src_prefix=None, part="all", alt0=False, pre_k=1,
                           delay1=False):
            blocks = list(blocks)

            def src_rb(b):
                return [buf(f"{src_prefix}{b}")] if src_prefix else []

            def loads(bi, b):
                s, n = BLOCKS[b]
                par = (bi + poff) % 2
                xa = A_xin[:, par * CT * 512: par * CT * 512 + CT * n]
                src_ = d_in[f"x0_{b}"] if srcsp is None else srcsp[b]
                w_op("sync", lambda e, xa=xa, src_=src_: e.dma_start(out=xa, in_=src_),
                     [buf(f"xin{par}")], src_rb(b), inc=16)

            def ln(bi, b):
                s, n = BLOCKS[b]
                par = (bi + poff) % 2
                xa = A_xin[:, par * CT * 512: par * CT * 512 + CT * n]
                xa3 = xa.rearrange("p (c n) -> p c n", c=CT)
                if alt0 and bi == 0:
                    xb = A_u2[:, 0:CT * n]
                    xb_b = buf("A_u2")
                else:
                    xb = A_srw[:, 0:CT * n]
                    xb_b = buf("A_srw")
                src_ = d_in[f"x0_{b}"] if srcsp is None else srcsp[b]
                w_op("gpsimd", lambda e, xb=xb, src_=src_: e.dma_start(
                        out=xb, in_=src_),
                     [xb_b], src_rb(b), inc=16)
                uqs = uq_slots[par][:, 0:CT * n]
                uq3 = uqs.rearrange("p (c n) -> p c n", c=CT)
                h = CT // 2
                tmp3 = ab_tmp[:, 0:h * n].rearrange("p (c n) -> p c n", c=h)
                plan_ln(xa, xa3, buf(f"xin{par}"), n, xb, xb_b,
                        [uq3[:, 0:h, :], uq3[:, h:CT, :]], buf(f"uq{par}"),
                        [tmp3, tmp3], buf("ab_tmp"), skip_cast=True)

            def body(bi, b):
                s, n = BLOCKS[b]
                par = (bi + poff) % 2
                xa = A_xin[:, par * CT * 512: par * CT * 512 + CT * n]
                w_op("sync", lambda e, n=n, b=b: e.dma_start(
                        out=A_msk[:, 0:CT * n], in_=d_in[f"mk_{b}"]),
                     [buf("A_msk")], [], inc=16)
                ybv = A_yb[:]
                w_op("vector", lambda e, xa=xa, n=n: e.tensor_copy(
                        out=stage_ab[:, 0:CT * n], in_=xa),
                     [buf("stage")], [buf(f"xin{par}")])
                plan_ffn(n, ybv, uq_slots[par][:, 0:CT * n], buf(f"uq{par}"))
                dv = A_d[:]
                w_op("vector", lambda e, n=n: e.tensor_tensor(
                        out=dv[:, 0:CT * n], in0=ybv[:, 0:CT * n],
                        in1=A_msk[:, 0:CT * n], op=ALU.mult),
                     [buf("A_d")], [buf("A_yb"), buf("A_msk")])
                w_op("vector", lambda e, n=n: e.tensor_tensor(
                        out=stage_ab[:, 0:CT * n], in0=stage_ab[:, 0:CT * n],
                        in1=dv[:, 0:CT * n], op=ALU.add),
                     [buf("stage")], [buf("stage"), buf("A_d")])
                dst = dstsp[b]
                w_op("scalar", lambda e, dst=dst, n=n: e.dma_start(
                        out=dst, in_=stage_ab[:, 0:CT * n]),
                     [buf(f"{dr_prefix}{b}")] if dr_prefix else [],
                     [buf("stage")], sname="sts", inc=16)

            if part in ("pre", "all"):
                for i in range(min(pre_k, len(blocks))):
                    loads(i, blocks[i])
                    ln(i, blocks[i])
            if part == "pre":
                return
            for i, b in enumerate(blocks):
                j = i + 1
                emit_lead = j < len(blocks) and j >= pre_k
                if emit_lead and not (delay1 and i == 0):
                    loads(j, blocks[j])
                    ln(j, blocks[j])
                body(i, b)
                if emit_lead and delay1 and i == 0:
                    loads(j, blocks[j])
                    ln(j, blocks[j])

        cde_prev_n = [None]

        def plan_cde(wv1, wv2, part="all", wo_cast=None):
            def a_load(b):
                s, n = BLOCKS[b]
                par = b % 2
                xa = A_xin[:, par * CT * 512: par * CT * 512 + CT * n]
                w_op("sync", lambda e, xa=xa, b=b: e.dma_start(out=xa, in_=x1sp[b]),
                     [buf(f"xin{par}")], [buf(f"x1d_{b}")], inc=16)

            def a_shadow(b):
                s, n = BLOCKS[b]
                xb = A_srw[:, 0:CT * n]
                w_op("gpsimd", lambda e, xb=xb, b=b: e.dma_start(
                        out=xb, in_=x1sp[b]),
                     [buf("A_srw")], [buf(f"x1d_{b}")], inc=16)

            if part == "pre":
                a_load(0)
                a_load(1)
                a_shadow(0)
                return
            WkT = wv1
            WvT_off = CT * C
            WrT = wv2
            WoT_off = CT * C
            srb = A_yb[:]
            srw = A_srw[:]
            kv16 = A_kv[:].bitcast(mybir.dt.float16)
            kvb = A_kv[:].bitcast(BF16)
            hqb = A_hq[:].bitcast(BF16)
            FP16 = mybir.dt.float16

            def kslot(b):
                p = b % 2
                return kv16[:, p * CT * 512:(p + 1) * CT * 512]

            def vslot(b):
                p = b % 2
                return kvb[:, 2 * CT * 512 + p * CT * 512:
                           2 * CT * 512 + (p + 1) * CT * 512]

            mixk = A_xsq[:]
            mixv_ = hqb[:, 0:CT * 512]
            mixr = hqb[:, CT * 512:2 * CT * 512]
            mix_of = {"k": (mixk, buf("A_xsq")), "v": (mixv_, buf("mixv")),
                      "r": (mixr, buf("mixr"))}


            nb = len(BLOCKS)
            prev_n = cde_prev_n

            def stageA(b):
                s, n = BLOCKS[b]
                par = b % 2
                xa = A_xin[:, par * CT * 512: par * CT * 512 + CT * n]
                xbf = buf(f"xin{par}")
                if b >= 2:
                    a_load(b)
                if prev_n[0] is not None:
                    w_op("vector", lambda e, pn=prev_n[0]: e.tensor_copy(
                            out=u2v[:, :, 0:1], in_=u2v[:, :, pn:pn + 1]),
                         [buf("A_u2")], [buf("A_u2")])
                prev_n[0] = n
                xb = A_srw[:, 0:CT * n]
                xa3 = xa.rearrange("p (c n) -> p c n", c=CT)
                h = CT // 2
                ctmp = A_wk[:, 0:h * 512][:, 0:h * n] if False else \
                    A_wk[:, 0:2048][:, 0:h * n].rearrange("p (c n) -> p c n", c=h)
                wkbufs = [buf("wk0"), buf("wk1"), buf("wk2"), buf("wk3")]
                if b >= 1:
                    a_shadow(b)
                xsq = mixk[:, 0:CT * n]
                w_op("scalar", lambda e, xsq=xsq, xb=xb: e.activation(
                        out=xsq, in_=xb, func=AF.Square),
                     [buf("A_xsq")], [buf("A_srw")])
                def stats_group(psd, psb, srcbuf, src):
                    waits = list(srcbuf.rdeps()) + list(buf("ones_bf").rdeps()) + \
                        list(srcbuf.writers) + psb.wdeps()
                    tok = None
                    for ct in range(CT):
                        fn = (lambda e, ct=ct: e.matmul(
                            psd[0:1, 0:n], ones_bf[:], src[:, ct * n:(ct + 1) * n],
                            start=(ct == 0), stop=(ct == CT - 1)))
                        if ct == 0:
                            P.op("tensor", fn, waits, inc=0)
                        elif ct == CT - 1:
                            tok = P.op("tensor", fn, (), inc=1)
                        else:
                            P.op("tensor", fn, (), inc=0)
                    srcbuf.addr(tok)
                    buf("ones_bf").addr(tok)
                    psb.setw(tok)
                stats_group(ps_s1, buf("ps_s1"), buf("A_srw"), xb)
                stats_group(ps_s2, buf("ps_s2"), buf("A_xsq"), xsq)
                mrow = WKB[0][0:1, 0:n]
                erow = WKB[1][0:1, 0:n]
                m2 = WKB[2][0:1, 0:n]
                w_op("vector", lambda e, mrow=mrow: e.tensor_scalar(
                        out=mrow, in0=ps_s1[0:1, 0:n], scalar1=float(1.0 / C),
                        scalar2=None, op0=ALU.mult),
                     [buf("wk0")], [buf("ps_s1")])
                w_op("vector", lambda e, erow=erow: e.tensor_scalar(
                        out=erow, in0=ps_s2[0:1, 0:n], scalar1=float(1.0 / C),
                        scalar2=None, op0=ALU.mult),
                     [buf("wk1")], [buf("ps_s2")])
                w_op("vector", lambda e, m2=m2, mrow=mrow: e.tensor_tensor(
                        out=m2, in0=mrow, in1=mrow, op=ALU.mult),
                     [buf("wk2")], [buf("wk0")])
                w_op("vector", lambda e, erow=erow, m2=m2: e.tensor_tensor(
                        out=erow, in0=erow, in1=m2, op=ALU.subtract),
                     [buf("wk1")], [buf("wk1"), buf("wk2")])
                w_op("vector", lambda e, erow=erow: e.tensor_scalar(
                        out=erow, in0=erow, scalar1=1e-5, scalar2=None, op0=ALU.add),
                     [buf("wk1")], [buf("wk1")])
                w_op("scalar", lambda e, erow=erow: e.activation(
                        out=erow, in_=erow, func=AF.Ln),
                     [buf("wk1")], [buf("wk1")])
                w_op("scalar", lambda e, erow=erow: e.activation(
                        out=erow, in_=erow, func=AF.Exp, scale=-0.5),
                     [buf("wk1")], [buf("wk1")])
                w_op("tensor", lambda e, mrow=mrow: e.matmul(
                        ps_bc1[:, 0:n], ones_k1[:], mrow, start=True, stop=True),
                     [buf("ps_bc1")], [buf("wk0"), buf("ones_k1")])
                w_op("tensor", lambda e, erow=erow: e.matmul(
                        ps_bc2[:, 0:n], ones_k1[:], erow, start=True, stop=True),
                     [buf("ps_bc2")], [buf("wk1"), buf("ones_k1")])
                for g in range(2):
                    xs = xa3[:, g * h:(g + 1) * h, :]
                    dst = u2v[:, g * h:(g + 1) * h, 1:n + 1]
                    bc1 = ps_bc1[:, 0:n].unsqueeze(1).broadcast_to([128, h, n])
                    bc2 = ps_bc2[:, 0:n].unsqueeze(1).broadcast_to([128, h, n])
                    w_op("vector", lambda e, xs=xs, bc1=bc1, ctmp=ctmp: e.tensor_tensor(
                            out=ctmp, in0=xs, in1=bc1, op=ALU.subtract),
                         wkbufs, [xbf, buf("ps_bc1")])
                    w_op("vector", lambda e, dst=dst, bc2=bc2, ctmp=ctmp: e.tensor_tensor(
                            out=dst, in0=ctmp, in1=bc2, op=ALU.mult),
                         [buf("A_u2")], wkbufs + [buf("ps_bc2")])
                dvv = A_d[:].rearrange("p (c n) -> p c n", c=CT)
                w_op("vector", lambda e, n=n, dvv=dvv: e.tensor_tensor(
                        out=dvv[:, :, 0:n], in0=u2v[:, :, 1:n + 1],
                        in1=u2v[:, :, 0:n], op=ALU.subtract),
                     [buf("A_d")], [buf("A_u2")])
                for pnm, tmc, tmb in (("k", c_tmk, buf("tmkc")),
                                      ("v", c_tmv, buf("tmvc")),
                                      ("r", c_tmr, buf("tmrc"))):
                    mv, mb = mix_of[pnm]
                    m3 = mv[:, 0:CT * n].rearrange("p (c n) -> p c n", c=CT)
                    for ct in range(CT):
                        w_op("scalar", lambda e, ct=ct, tmc=tmc, n=n, mv=mv, dvv=dvv:
                             e.activation(out=mv[:, ct * n:(ct + 1) * n],
                                          in_=dvv[:, ct, 0:n], func=AF.Identity,
                                          scale=tmc[:, ct:ct + 1]),
                             [mb], [buf("A_d"), tmb])
                    w_op("vector", lambda e, m3=m3, n=n: e.tensor_tensor(
                            out=m3, in0=m3, in1=u2v[:, :, 0:n], op=ALU.add),
                         [mb], [mb, buf("A_u2")])

            def _proj_mms(pnm, j, b, n, psd, psb):
                WT, woff, wtb = {"k": (WkT, 0, buf("A_w1")),
                                 "v": (WkT, WvT_off, buf("A_w1b")),
                                 "r": (WrT, 0, buf("A_w2"))}[pnm]
                mv, mb = mix_of[pnm]
                waits = mb.rdeps() + wtb.rdeps() + psb.wdeps()
                tok = None
                for ct in range(CT):
                    fn = (lambda e, ct=ct, j=j, psd=psd, WT=WT, woff=woff,
                          mv=mv, n=n: e.matmul(
                        psd[:, 0:n],
                        WT[:, woff + ct * C + j * 128:
                           woff + ct * C + (j + 1) * 128],
                        mv[:, ct * n:(ct + 1) * n],
                        start=(ct == 0), stop=(ct == CT - 1)))
                    if ct == 0:
                        P.op("tensor", fn, waits, inc=0)
                    elif ct == CT - 1:
                        tok = P.op("tensor", fn, (), inc=1)
                    else:
                        P.op("tensor", fn, (), inc=0)
                mb.addr(tok)
                wtb.addr(tok)
                psb.setw(tok)

            def stageB_kv_tiles(b, js):
                s, n = BLOCKS[b]
                ks = kslot(b)
                vs = vslot(b)
                p = b % 2
                for j in js:
                    for pnm in ("k", "v"):
                            psd = ps_mm[0 if pnm == "k" else 1]
                            psb = buf("ps_mm0" if pnm == "k" else "ps_mm1")
                            _proj_mms(pnm, j, b, n, psd, psb)
                            if pnm == "k":
                                if b == 0:
                                    fn2 = (lambda e, j=j, psd=psd, n=n, ks=ks:
                                           e.activation(
                                        out=ks[:, j * 512:j * 512 + n],
                                        in_=psd[:, 0:n], func=AF.Identity,
                                        bias=c_kb[:, j:j + 1], scale=1.0))
                                    rb = [psb, buf("kbc")]
                                else:
                                    fn2 = (lambda e, j=j, psd=psd, n=n, ks=ks:
                                           e.activation(
                                        out=ks[:, j * 512:j * 512 + n],
                                        in_=psd[:, 0:n], func=AF.Identity,
                                        scale=1.0))
                                    rb = [psb]
                            else:
                                fn2 = (lambda e, j=j, psd=psd, n=n, vs=vs:
                                       e.activation(
                                    out=vs[:, j * 512:j * 512 + n],
                                    in_=psd[:, 0:n], func=AF.Identity,
                                    scale=1.0))
                                rb = [psb]
                            w_op("scalar", fn2, [buf(f"{pnm}par{p}j{j}")], rb)

            def stageB(b, which=("k", "v", "r")):
                s, n = BLOCKS[b]
                if "k" in which or "v" in which:
                    stageB_kv_tiles(b, range(CT))
                if "r" in which:
                    for j in range(CT):
                        psd = ps_mm[j % 2]
                        psb = buf(f"ps_mm{j % 2}")
                        _proj_mms("r", j, b, n, psd, psb)
                        fn2 = (lambda e, j=j, psd=psd, n=n: e.activation(
                            out=srb[:, j * 512:j * 512 + n], in_=psd[:, 0:n],
                            func=AF.Sigmoid, scale=1.0))
                        w_op("scalar", fn2, [buf("dst_r")], [psb])

            def stageC(b, interleave=None):
                s, n = BLOCKS[b]
                ks = kslot(b)
                vs = vslot(b)
                pp = b % 2
                rb_ = buf("dst_r")
                dW = A_d[:].bitcast(F32)
                sets = []
                for si in range(2):
                    if si == 0:
                        wk = WKB
                        wkb = [buf(f"wk{i}") for i in range(4)]
                        pab = mskf[:, 0:1539]
                        pabb = buf("pabA")
                    else:
                        wk = [dW[:, i * 512:(i + 1) * 512] for i in range(4)]
                        wkb = [buf(f"wkB{i}") for i in range(4)]
                        pab = A_pab2[:]
                        pabb = buf("pabB")
                    sets.append((wk, wkb, pab, pabb))

                def chain_steps(ct):
                    wk, wkb, pab, pabb = sets[ct % 2]
                    kb_ = buf(f"kpar{pp}j{ct}")
                    vb_ = buf(f"vpar{pp}j{ct}")
                    Ap = pab[:, 0:513]
                    Aa = pab[:, 513:1026]
                    Ab = pab[:, 1026:1539]
                    kap = ks[:, ct * 512:ct * 512 + n]
                    vap = vs[:, ct * 512:ct * 512 + n]
                    st3o = pab[:, 0:1539:513]
                    st3i = st_all[:, ct:ct + 2 * CT + 1:CT]
                    pe3o = pab[:, n:n + 1027:513]
                    steps = []
                    steps.append(("vector", lambda e: e.tensor_copy(out=st3o, in_=st3i),
                                  [pabb], [buf("st_all")]))
                    steps.append(("vector", lambda e: e.tensor_tensor_scan(
                            out=Ap[:, 1:n + 1],
                            data0=c_w[:, ct:ct + 1].broadcast_to([128, n]),
                            data1=kap, initial=st_all[:, ct:ct + 1],
                            op0=ALU.add, op1=ALU.max),
                        [pabb], [kb_, buf("c_w"), buf("st_all")]))
                    steps.append(("vector", lambda e: e.tensor_tensor(
                            out=wk[0][:, 0:n], in0=Ap[:, 0:n], in1=Ap[:, 1:n + 1],
                            op=ALU.subtract),
                        [wkb[0]], [pabb]))
                    steps.append(("vector", lambda e: e.tensor_tensor(
                            out=wk[2][:, 0:n], in0=kap, in1=Ap[:, 1:n + 1],
                            op=ALU.subtract),
                        [wkb[2]], [kb_, pabb]))
                    steps.append(("scalar", lambda e: e.activation(
                            out=wk[1][:, 0:n], in_=wk[0][:, 0:n], func=AF.Exp,
                            bias=c_w[:, ct:ct + 1], scale=1.0),
                        [wkb[1]], [wkb[0], buf("c_w")]))
                    steps.append(("scalar", lambda e: e.activation(
                            out=wk[3][:, 0:n], in_=wk[2][:, 0:n], func=AF.Exp),
                        [wkb[3]], [wkb[2]]))
                    steps.append(("vector", lambda e: e.tensor_tensor(
                            out=wk[0][:, 0:n], in0=wk[3][:, 0:n], in1=vap,
                            op=ALU.mult),
                        [wkb[0]], [wkb[3], vb_]))
                    steps.append(("vector", lambda e: e.tensor_tensor_scan(
                            out=Aa[:, 1:n + 1], data0=wk[1][:, 0:n],
                            data1=wk[0][:, 0:n], initial=st_all[:, CT + ct:CT + ct + 1],
                            op0=ALU.mult, op1=ALU.add),
                        [pabb], [wkb[1], wkb[0], buf("st_all")]))
                    steps.append(("vector", lambda e: e.tensor_tensor_scan(
                            out=Ab[:, 1:n + 1], data0=wk[1][:, 0:n],
                            data1=wk[3][:, 0:n], initial=st_all[:, 2 * CT + ct:2 * CT + ct + 1],
                            op0=ALU.mult, op1=ALU.add),
                        [pabb], [wkb[1], wkb[3], buf("st_all")]))
                    steps.append(("vector", lambda e: e.tensor_copy(out=st3i, in_=pe3o),
                                  [buf("st_all")], [pabb]))
                    if b == 0:
                        return steps
                    steps.append(("vector", lambda e: e.tensor_tensor(
                            out=wk[2][:, 0:n], in0=kap, in1=Ap[:, 0:n],
                            op=ALU.subtract),
                        [wkb[2]], [kb_, pabb]))
                    steps.append(("vector", lambda e: e.tensor_scalar(
                            out=wk[0][:, 0:n], in0=wk[2][:, 0:n], scalar1=20.0,
                            scalar2=None, op0=ALU.min),
                        [wkb[0]], [wkb[2]]))
                    steps.append(("scalar", lambda e: e.activation(
                            out=wk[2][:, 0:n], in_=wk[0][:, 0:n], func=AF.Exp,
                            bias=c_u[:, ct:ct + 1], scale=1.0),
                        [wkb[2]], [wkb[0], buf("c_u")]))
                    steps.append(("vector", lambda e: e.tensor_tensor(
                            out=wk[0][:, 0:n], in0=wk[2][:, 0:n], in1=vap,
                            op=ALU.mult),
                        [wkb[0]], [wkb[2], vb_]))
                    steps.append(("vector", lambda e: e.tensor_tensor(
                            out=wk[3][:, 0:n], in0=Aa[:, 0:n], in1=wk[0][:, 0:n],
                            op=ALU.add),
                        [wkb[3]], [pabb, wkb[0]]))
                    steps.append(("vector", lambda e: e.tensor_tensor(
                            out=wk[0][:, 0:n], in0=Ab[:, 0:n], in1=wk[2][:, 0:n],
                            op=ALU.add),
                        [wkb[0]], [pabb, wkb[2]]))
                    steps.append(("scalar", lambda e: e.activation(
                            out=wk[2][:, 0:n], in_=wk[0][:, 0:n], func=AF.Ln),
                        [wkb[2]], [wkb[0]]))
                    steps.append(("scalar", lambda e: e.activation(
                            out=wk[0][:, 0:n], in_=wk[2][:, 0:n], func=AF.Exp,
                            scale=-1.0),
                        [wkb[0]], [wkb[2]]))
                    steps.append(("vector", lambda e: e.tensor_tensor(
                            out=wk[2][:, 0:n], in0=wk[3][:, 0:n], in1=wk[0][:, 0:n],
                            op=ALU.mult),
                        [wkb[2]], [wkb[3], wkb[0]]))
                    steps.append(("vector", lambda e, ct=ct: e.tensor_tensor(
                            out=srw[:, ct * 512:ct * 512 + n], in0=wk[2][:, 0:n],
                            in1=srb[:, ct * 512:ct * 512 + n], op=ALU.mult),
                        [buf("A_srw"), buf(f"srwj{ct}")], [wkb[2], rb_]))
                    return steps

                for c0 in range(0, CT, 2):
                    if interleave is not None:
                        interleave(c0 // 2)
                    s0 = chain_steps(c0)
                    s1 = chain_steps(c0 + 1)
                    for st0, st1 in zip(s0, s1):
                        for eng, fn, wb, rb in (st0, st1):
                            w_op(eng, fn, wb, rb)
                tokA = ("v", P.cnt["v"])
                buf("A_d").addr(tokA)
                tokS = ("s", P.cnt["s"])
                buf("A_d").addr(tokS)

            def stageD(b):
                if b == 0:
                    return
                s, n = BLOCKS[b]
                par = b % 2
                xa = A_xin[:, par * CT * 512: par * CT * 512 + CT * n]
                xbf = buf(f"xin{par}")
                srwb = buf("A_srw")
                for j in range(CT):
                    psd = ps_mm[2 + j % 2]
                    psb = buf(f"ps_mm{2 + j % 2}")
                    waits = buf("A_w2b").rdeps() + psb.wdeps()
                    tok = None
                    for ct in range(CT):
                        fn = (lambda e, ct=ct, j=j, psd=psd, n=n: e.matmul(
                            psd[:, 0:n],
                            wv2[:, WoT_off + ct * C + j * 128:
                                WoT_off + ct * C + (j + 1) * 128],
                            srw[:, ct * 512:ct * 512 + n],
                            start=(ct == 0), stop=(ct == CT - 1)))
                        w = list(buf(f"srwj{ct}").rdeps())
                        if ct == 0:
                            P.op("tensor", fn, waits + w, inc=0)
                        elif ct == CT - 1:
                            tok = P.op("tensor", fn, w, inc=1)
                        else:
                            P.op("tensor", fn, w, inc=0)
                    srwb.addr(tok)
                    buf("A_w2b").addr(tok)
                    psb.setw(tok)
                    w_op("vector", lambda e, j=j, psd=psd, xa=xa, n=n: e.tensor_tensor(
                            out=xa[:, j * n:(j + 1) * n], in0=psd[:, 0:n],
                            in1=xa[:, j * n:(j + 1) * n], op=ALU.add),
                         [xbf], [psb, xbf])
                w_op("scalar", lambda e, b=b, n=n, xa=xa: e.dma_start(
                        out=x2sp[b], in_=xa[:, 0:CT * n]),
                     [buf(f"x2d_{b}")], [xbf], sname="sts", inc=16)

            nbk = len(BLOCKS)
            if part == "pre2":
                link(buf("mixv"), buf("A_hq"))
                link(buf("mixr"), buf("A_hq"))
                link(buf("dst_r"), buf("A_yb"))
                for j in range(CT):
                    link(buf(f"kpar0j{j}"), buf("stage"))
                    link(buf(f"vpar0j{j}"), buf("stage"))
                    link(buf(f"vpar0j{j}"), buf("uq0"))
                    link(buf(f"vpar0j{j}"), buf("uq1"))
                for ct in range(CT):
                    w_op("vector", lambda e, ct=ct: e.memset(u2v[:, ct, 0:1], 0.0),
                         [buf("A_u2")], [])
                stageA(0)
                stageB(0)
                stageA(1)
                return
            if wo_cast is not None:
                wo_cast()
            for b in range(1, nbk):
                stageC(b - 1,
                       interleave=lambda jp, b=b: stageB_kv_tiles(
                           b, (2 * jp, 2 * jp + 1)))
                stageD(b - 1)
                stageB(b, which=("r",))
                if b + 1 < nbk:
                    stageA(b + 1)
            stageC(nbk - 1)
            stageD(nbk - 1)

        def barrier(snap=None):
            toks = snap if snap is not None else \
                [(sn, P.cnt[sn]) for sn in
                 ("lds0", "lds1", "sts0", "sts1", "ldg0", "ldg1", "t", "v", "s")]
            for e in ENGS:
                for sn, v in toks:
                    if v > 0 and P.sem_owner[sn] != e:
                        P.raw_wait(e, sn, v)

        def snap_counts():
            return [(sn, P.cnt[sn]) for sn in
                    ("lds0", "lds1", "sts0", "sts1", "ldg0", "ldg1",
                     "t", "v", "s")]

        prologue_memsets()
        plan_ffn_phase(None, x1sp, range(len(BLOCKS)), dr_prefix="x1d_",
                       part="pre", pre_k=2)
        prologue_consts()
        cast_ffn_weights()
        plan_ffn_phase(None, x1sp, range(len(BLOCKS)), dr_prefix="x1d_",
                       part="rest", pre_k=2)
        _snap0 = snap_counts()
        plan_cde(None, None, part="pre")
        wv1, wv2 = cast_attn_weights_kvr()
        plan_cde(wv1, wv2, part="pre2")
        barrier(_snap0)
        plan_cde(wv1, wv2, part="rest", wo_cast=cast_attn_weights_o)
        _snap = snap_counts()
        for _j in range(CT):
            link(buf("uq1"), buf(f"vpar0j{_j}"))
            link(buf("ab_tmp"), buf(f"vpar1j{_j}"))
        plan_ffn_phase(x2sp, outs, range(1, len(BLOCKS)), poff=1,
                       dr_prefix="outd_", src_prefix="x2d_", part="pre",
                       alt0=True)
        cast_ffn_weights()
        barrier(_snap)
        plan_ffn_phase(x2sp, outs, range(1, len(BLOCKS)), poff=1,
                       dr_prefix="outd_", src_prefix="x2d_", part="rest",
                       alt0=True, delay1=True)
        P.raw_wait("scalar", "sts0", P.cnt["sts0"])
        P.raw_wait("scalar", "sts1", P.cnt["sts1"])
        P.raw_wait("gpsimd", "sts0", P.cnt["sts0"])
        P.raw_wait("gpsimd", "sts1", P.cnt["sts1"])

        with nc.Block() as block:
            @block.sync
            def _(e):
                for f in P.q["sync"]:
                    f(e)

            @block.scalar
            def _(e):
                for f in P.q["scalar"]:
                    f(e)

            @block.vector
            def _(e):
                for f in P.q["vector"]:
                    f(e)

            @block.tensor
            def _(e):
                for f in P.q["tensor"]:
                    f(e)

            @block.gpsimd
            def _(e):
                for f in P.q["gpsimd"]:
                    f(e)

    return nc


_NC_CACHE = {}


def kernel(L0_attn_x, L0_attn_time_decay, L0_attn_time_first, L0_attn_time_mix_k,
           L0_attn_time_mix_v, L0_attn_time_mix_r, L0_attn_Wk, L0_attn_Wv,
           L0_attn_Wr, L0_attn_Wo, L0_ffn_W1, L0_ffn_b1, L0_ffn_W2, L0_ffn_b2):
    x = np.asarray(L0_attn_x, dtype=np.float32).reshape(16384, 1024)
    xT = np.ascontiguousarray(x.T)
    maskT = np.ascontiguousarray(_mk_mask625().reshape(16384, 1024).T)

    w1t = _prep_weightT(L0_ffn_W1)
    w2t = _prep_weightT(L0_ffn_W2)
    wkt = _prep_weightT(L0_attn_Wk)
    wvt = _prep_weightT(L0_attn_Wv)
    wrt = _prep_weightT(L0_attn_Wr)
    wot = _prep_weightT(L0_attn_Wo)
    b1c = _colN(L0_ffn_b1, HT)
    b2c = _colN(L0_ffn_b2, CT)
    tdc = _colN(np.asarray(L0_attn_time_decay).reshape(-1), CT)
    tfc = _colN(np.asarray(L0_attn_time_first).reshape(-1), CT)
    tmkc = _colN(np.asarray(L0_attn_time_mix_k).reshape(-1), CT)
    tmvc = _colN(np.asarray(L0_attn_time_mix_v).reshape(-1), CT)
    tmrc = _colN(np.asarray(L0_attn_time_mix_r).reshape(-1), CT)

    if "nc" not in _NC_CACHE:
        _NC_CACHE["nc"] = build_program()
    nc = _NC_CACHE["nc"]

    in_maps = []
    for core in range(NCORES):
        S0 = core * TLOC
        xs = np.zeros((1024, TTOT), np.float32)
        ms = np.zeros((1024, TTOT), np.float32)
        lo = S0 - HALO
        src_lo = max(lo, 0)
        off = src_lo - lo
        xs[:, off:] = xT[:, src_lo:S0 + TLOC]
        ms[:, off:] = maskT[:, src_lo:S0 + TLOC]
        kb = np.zeros((128, CT), np.float32)
        if core == 0:
            kb[:] = -60000.0
        cst = np.ascontiguousarray(np.concatenate(
            [b1c, b2c, tdc, tfc, tmkc, tmvc, tmrc, kb], axis=1))
        im = {"w1t": w1t, "w2t": w2t, "wkt": wkt, "wvt": wvt, "wrt": wrt,
              "wot": wot, "cst": cst}
        for b, (s, n) in enumerate(BLOCKS):
            im[f"x0_{b}"] = _tile_cols(xs, s, n, np.float32)
            im[f"mk_{b}"] = _tile_cols(ms, s, n, np.float32).astype(ml_dtypes.bfloat16)
        in_maps.append(im)

    res = run_bass_kernel_spmd(nc, in_maps, core_ids=list(range(NCORES)))

    out = np.empty((16384, 1024), np.float32)
    for core in range(NCORES):
        S0 = core * TLOC
        r = res.results[core]
        for b, (s, n) in enumerate(BLOCKS):
            if b == 0:
                continue
            blk = np.asarray(r[f"x3_{b}"]).reshape(128, CT, n).transpose(1, 0, 2)
            g = S0 - HALO + s
            out[g:g + n, :] = blk.reshape(1024, n).T
    return out.reshape(1, 16384, 1024)


if __name__ == "__main__":
    nc = build_program()
    n_inst = sum(len(v) for v in nc.to_json().get("functions", [{}])[0].get(
        "instructions", [])) if False else "?"
    print("build ok")


# version 46
# speedup vs baseline: 1.0154x; 1.0154x over previous
import contextlib
import numpy as np
import concourse.bass as bass
import concourse.mybir as mybir
from concourse.bass_utils import run_bass_kernel_spmd
import ml_dtypes

F32 = mybir.dt.float32
BF16 = mybir.dt.bfloat16
FP8 = mybir.dt.float8e5
AF = mybir.ActivationFunctionType
ALU = mybir.AluOpType

C = 1024
CT = 8
DFF = 4096
HT = 32
NCORES = 8
TLOC = 2048
HALO = 40
TTOT = TLOC + HALO
BLOCKS = [(0, 40), (40, 512), (552, 512), (1064, 512), (1576, 512)]

ENGS = ("sync", "scalar", "vector", "tensor", "gpsimd")
SERIAL = False


class Plan:
    def __init__(self, nc, stack):
        self.nc = nc
        self.q = {e: [] for e in ENGS}
        self.sem = {}
        self.cnt = {}
        for name in ("lds0", "lds1", "sts0", "sts1", "ldg0", "ldg1",
                     "wc0", "wc1", "t", "v", "s", "g"):
            self.sem[name] = stack.enter_context(nc.semaphore(name))
            self.cnt[name] = 0
        self.eng_sem = {"sync": "lds0", "scalar": "s", "vector": "v",
                        "tensor": "t", "gpsimd": "g"}
        self.sem_owner = {"lds0": "sync", "lds1": "sync", "sts0": "scalar",
                          "sts1": "scalar", "ldg0": "gpsimd", "ldg1": "gpsimd",
                          "wc0": "gpsimd", "wc1": "gpsimd",
                          "t": "tensor", "v": "vector", "s": "scalar",
                          "g": "gpsimd"}
        self.dma_rr = {"lds": 0, "sts": 0, "ldg": 0, "wc": 0}
        self.seen = {e: {} for e in ENGS}

    def dma(self, eng, fn, pool, rbufs_toks=()):
        i = self.dma_rr[pool]
        self.dma_rr[pool] = 1 - i
        sname = f"{pool}{i}"
        waits = list(rbufs_toks) + [(sname, self.cnt[sname])]
        return self.op(eng, fn, waits, sname=sname, inc=16)

    def raw_wait(self, eng, sname, val):
        if val <= 0 or self.seen[eng].get(sname, 0) >= val:
            return
        self.seen[eng][sname] = val
        h = self.sem[sname]
        self.q[eng].append(lambda e, h=h, val=val: e.wait_ge(h, val))

    def op(self, eng, fn, waits=(), sname=None, inc=1):
        if SERIAL:
            waits = list(waits) + [(sn, self.cnt[sn]) for sn in
                                   ("lds0", "lds1", "sts0", "sts1", "ldg0",
                                    "ldg1", "t", "v", "s")]
        best = {}
        for sn, v in waits:
            if v and v > best.get(sn, 0):
                best[sn] = v
        for sn, v in best.items():
            self.raw_wait(eng, sn, v)
        if sname is None:
            sname = self.eng_sem[eng]
        self.cnt[sname] += inc
        val = self.cnt[sname]
        h = self.sem[sname]

        def build(e, fn=fn, h=h, inc=inc):
            r = fn(e)
            if inc:
                r.then_inc(h, inc)
        self.q[eng].append(build)
        return (sname, val)


class Buf:
    def __init__(self, name):
        self.name = name
        self.writers = []
        self.readers = []

    def wdeps(self):
        return list(self.readers) + list(self.writers)

    def rdeps(self):
        return list(self.writers)

    def setw(self, tok):
        self.writers = list(tok) if isinstance(tok, list) else [tok]
        self.readers = []

    def addr(self, tok):
        self.readers.append(tok)


def _threefry2x32(k0, k1, x0, x1):
    rot = [13, 15, 26, 6, 17, 29, 16, 24]
    x0 = x0.astype(np.uint32).copy()
    x1 = x1.astype(np.uint32).copy()
    ks = [np.uint32(k0), np.uint32(k1),
          np.uint32(np.uint32(k0) ^ np.uint32(k1) ^ np.uint32(0x1BD11BDA))]
    x0 += ks[0]
    x1 += ks[1]
    for i in range(5):
        for r in rot[4 * (i % 2):4 * (i % 2) + 4]:
            x0 += x1
            x1 = (x1 << np.uint32(r)) | (x1 >> np.uint32(32 - r))
            x1 ^= x0
        x0 += ks[(i + 1) % 3]
        x1 += ks[(i + 2) % 3] + np.uint32(i + 1)
    return x0, x1


def _mk_mask625():
    n = 16384 * 1024
    i64 = np.arange(n, dtype=np.uint64)
    h0, h1 = _threefry2x32(0, 42, (i64 >> np.uint64(32)).astype(np.uint32),
                           i64.astype(np.uint32))
    import os
    if os.environ.get("BASS_SIM_FP64"):
        b64 = (h0.astype(np.uint64) << np.uint64(32)) | h1.astype(np.uint64)
        fl64 = ((b64 >> np.uint64(12)) | np.uint64(0x3FF0000000000000)) \
            .view(np.float64) - 1.0
        keep = (fl64 < 0.8).reshape(16384, 1024)
    else:
        bits = h0 ^ h1
        fl = ((bits >> np.uint32(9)) | np.uint32(0x3F800000)).view(np.float32) \
            - np.float32(1.0)
        keep = (fl < np.float32(0.8)).reshape(16384, 1024)
    return np.where(keep, np.float32(0.625), np.float32(0.0)).astype(np.float32)


def _tile_cols(arrT, s, n, dtype):
    blk = arrT[:, s:s + n]
    nt = blk.shape[0] // 128
    out = blk.reshape(nt, 128, n).transpose(1, 0, 2).reshape(128, nt * n)
    return np.ascontiguousarray(out).astype(dtype)


def _prep_weightT(w):
    wT = np.ascontiguousarray(np.asarray(w, np.float32).T)
    nt = wT.shape[0] // 128
    return np.ascontiguousarray(
        wT.reshape(nt, 128, wT.shape[1]).transpose(1, 0, 2).reshape(128, nt * wT.shape[1]))


def _colN(vec, nt):
    return np.ascontiguousarray(np.asarray(vec, np.float32).reshape(nt, 128).T)


def build_program(debug=False):
    nc = bass.Bass("TRN2", target_bir_lowering=False, debug=False)

    d_in = {}
    for b, (s, n) in enumerate(BLOCKS):
        d_in[f"x0_{b}"] = nc.dram_tensor(f"x0_{b}", [128, CT * n], F32, kind="ExternalInput").ap()
        d_in[f"mk_{b}"] = nc.dram_tensor(f"mk_{b}", [128, CT * n], BF16, kind="ExternalInput").ap()
    d_in["w1t"] = nc.dram_tensor("w1t", [128, CT * DFF], F32, kind="ExternalInput").ap()
    d_in["w2t"] = nc.dram_tensor("w2t", [128, HT * C], F32, kind="ExternalInput").ap()
    for nm in ("wkt", "wvt", "wrt", "wot"):
        d_in[nm] = nc.dram_tensor(nm, [128, CT * C], F32, kind="ExternalInput").ap()
    d_in["cst"] = nc.dram_tensor("cst", [128, HT + 7 * CT], F32,
                                 kind="ExternalInput").ap()

    dbg = {}
    if debug:
        dbg["uq"] = nc.dram_tensor("dbg_uq", [128, CT * 128], FP8, kind="ExternalOutput").ap()
        dbg["hq"] = nc.dram_tensor("dbg_hq", [128, HT * 128], FP8, kind="ExternalOutput").ap()
        dbg["yb"] = nc.dram_tensor("dbg_yb", [128, CT * 128], BF16, kind="ExternalOutput").ap()
        dbg["xb"] = nc.dram_tensor("dbg_xb", [128, CT * 128], BF16, kind="ExternalOutput").ap()
        dbg["k1"] = nc.dram_tensor("dbg_k1", [128, CT * 512], F32, kind="ExternalOutput").ap()
        dbg["v1"] = nc.dram_tensor("dbg_v1", [128, CT * 512], F32, kind="ExternalOutput").ap()
        dbg["sr1"] = nc.dram_tensor("dbg_sr1", [128, CT * 512], BF16, kind="ExternalOutput").ap()
        dbg["srw1"] = nc.dram_tensor("dbg_srw1", [128, CT * 512], BF16, kind="ExternalOutput").ap()
        dbg["pab"] = nc.dram_tensor("dbg_pab", [128, 4 * 513], F32, kind="ExternalOutput").ap()
    spill_kind = "ExternalOutput" if debug else "Internal"
    x1sp, x2sp, outs = {}, {}, {}
    for b, (s, n) in enumerate(BLOCKS):
        x1sp[b] = nc.dram_tensor(f"x1sp_{b}", [128, CT * n], F32, kind=spill_kind).ap()
        if b > 0:
            x2sp[b] = nc.dram_tensor(f"x2sp_{b}", [128, CT * n], F32, kind=spill_kind).ap()
            outs[b] = nc.dram_tensor(f"x3_{b}", [128, CT * n], F32, kind="ExternalOutput").ap()

    with contextlib.ExitStack() as ST:
        def sb(name, shape, dt):
            return ST.enter_context(nc.sbuf_tensor(name, shape, dt))

        A_w1 = sb("A_w1", [128, CT * DFF], FP8)
        A_w2 = sb("A_w2", [128, HT * C], FP8)
        A_xin = sb("A_xin", [128, 2 * CT * 512], F32)
        A_kv = sb("A_kv", [128, 2 * CT * 512], F32)
        A_hq = sb("A_hq", [128, HT * 512], FP8)
        A_wk = sb("A_wk", [128, 4 * 512], F32)
        WKB = [A_wk[:, i * 512:(i + 1) * 512] for i in range(4)]
        A_msk = sb("A_msk", [128, CT * 512], BF16)
        A_yb = sb("A_yb", [128, CT * 512], BF16)
        A_d = sb("A_d", [128, CT * 512], BF16)
        A_xsq = sb("A_xsq", [128, CT * 512], BF16)
        A_srw = sb("A_srw", [128, CT * 512], BF16)
        A_u2 = sb("A_u2", [128, CT * 513], BF16)

        c_all = sb("c_all", [128, HT + 7 * CT], F32)
        c_b1 = c_all[:, 0:HT]
        c_b2 = c_all[:, HT:HT + CT]
        c_b1q8 = sb("c_b1q8", [128, HT], FP8)
        c_b2q8 = sb("c_b2q8", [128, CT], FP8)
        c_td = c_all[:, HT + CT:HT + 2 * CT]
        c_u = c_all[:, HT + 2 * CT:HT + 3 * CT]
        c_tmk = c_all[:, HT + 3 * CT:HT + 4 * CT]
        c_tmv = c_all[:, HT + 4 * CT:HT + 5 * CT]
        c_tmr = c_all[:, HT + 5 * CT:HT + 6 * CT]
        c_kb = c_all[:, HT + 6 * CT:HT + 7 * CT]
        c_w = sb("c_w", [128, CT], F32)
        st_all = sb("st_all", [128, 3 * CT], F32)
        A_pab2 = sb("A_pab2", [128, 1539], F32)
        ones_bf = sb("ones_bf", [128, 1], BF16)
        ones_k1 = sb("ones_k1", [1, 128], F32)

        ps_s1 = ST.enter_context(nc.psum_tensor("ps_s1", [1, 512], F32))
        ps_s2 = ST.enter_context(nc.psum_tensor("ps_s2", [1, 512], F32))
        ps_bc1 = ST.enter_context(nc.psum_tensor("ps_bc1", [128, 512], F32))
        ps_bc2 = ST.enter_context(nc.psum_tensor("ps_bc2", [128, 512], F32))
        ps_mm = [ST.enter_context(nc.psum_tensor(f"ps_mm{i}", [128, 512], F32))
                 for i in range(4)]

        P = Plan(nc, ST)
        B = {}

        def buf(name):
            if name not in B:
                B[name] = Buf(name)
            return B[name]

        def w_op(eng, fn, wbufs, rbufs, sname=None, inc=1):
            waits = []
            for bu in rbufs:
                waits += bu.rdeps()
            for bu in wbufs:
                waits += bu.wdeps()
            if inc == 16:
                tok = P.dma(eng, fn, {"sync": "lds", "scalar": "sts",
                                      "gpsimd": "ldg"}[eng], waits)
            else:
                tok = P.op(eng, fn, waits, sname=sname, inc=inc)
            for bu in rbufs:
                bu.addr(tok)
            for bu in wbufs:
                bu.setw(tok)
            return tok

        def link(dst, src):
            dst.readers += src.writers + src.readers

        u2v = A_u2[:].rearrange("p (c n) -> p c n", c=CT)
        kv8 = A_kv[:].bitcast(FP8)
        stage_ab = A_kv[:, 0:CT * 512]
        uq_slots = [kv8[:, 16384 + p * 4096: 16384 + (p + 1) * 4096]
                    for p in range(2)]
        ab_tmp = A_kv[:, 6144:8192]
        kbv = A_kv[:, 0:CT * 512]
        vbv = A_kv[:, CT * 512:2 * CT * 512]
        stage_cde = A_hq[:].bitcast(F32)
        mskf = A_msk[:].bitcast(F32)
        A_p = mskf[:, 0:513]
        A_a = mskf[:, 513:1026]
        A_b = mskf[:, 1026:1539]

        def prologue_memsets():
            w_op("vector", lambda e: e.memset(ones_bf[:], 1.0), [buf("ones_bf")], [])
            w_op("vector", lambda e: e.memset(ones_k1[:], 1.0), [buf("ones_k1")], [])
            w_op("vector", lambda e: e.memset(st_all[:, 0:CT], -1e38), [buf("st_all")], [])
            w_op("vector", lambda e: e.memset(st_all[:, CT:3 * CT], 0.0), [buf("st_all")], [])

        def prologue_consts():
            tok = P.dma("sync", lambda e: e.dma_start(out=c_all[:], in_=d_in["cst"]),
                        "lds", [])
            for nm in ("b1c", "b2c", "tdc", "tfc", "tmkc", "tmvc", "tmrc", "kbc"):
                buf(nm).setw(tok)
            w_op("scalar", lambda e: e.activation(out=c_w[:], in_=c_td, func=AF.Exp),
                 [buf("c_w")], [buf("tdc")])
            w_op("vector", lambda e: e.tensor_scalar(out=c_w[:], in0=c_w[:], scalar1=-1.0,
                                                     scalar2=None, op0=ALU.mult),
                 [buf("c_w")], [buf("c_w")])
            w_op("scalar", lambda e: e.copy(out=c_b1q8[:], in_=c_b1),
                 [buf("b1q8")], [buf("b1c")])
            w_op("scalar", lambda e: e.copy(out=c_b1, in_=c_b1q8[:]),
                 [buf("b1c")], [buf("b1q8")])
            w_op("scalar", lambda e: e.copy(out=c_b2q8[:], in_=c_b2),
                 [buf("b2q8")], [buf("b2c")])
            w_op("scalar", lambda e: e.copy(out=c_b2, in_=c_b2q8[:]),
                 [buf("b2c")], [buf("b2q8")])

        def cast_ffn_weights():
            for dst, srcn, bufs in (
                    (A_w1, "w1t", [buf("A_w1"), buf("A_w1b")]),
                    (A_w2, "w2t", [buf("A_w2"), buf("A_w2b")])):
                X = CT * DFF
                waits = []
                for bu in bufs:
                    waits += bu.wdeps()
                toks = []
                for h in range(2):
                    lo, hi = h * (X // 2), (h + 1) * (X // 2)
                    toks.append(P.dma(
                        "gpsimd",
                        lambda e, lo=lo, hi=hi, dst=dst, srcn=srcn: e.dma_start(
                            out=dst[:, lo:hi], in_=d_in[srcn][:, lo:hi]),
                        "ldg", waits))
                for bu in bufs:
                    bu.setw(list(toks))

        def _cast_dma(dst_ap, srcn, bu, pool):
            waits = bu.wdeps()
            tok = P.dma("gpsimd",
                        lambda e, dst_ap=dst_ap, srcn=srcn: e.dma_start(
                            out=dst_ap, in_=d_in[srcn]),
                        pool, waits)
            bu.setw(tok)

        def cast_attn_weights_kvr():
            v1 = A_w1[:].bitcast(BF16)
            v2 = A_w2[:].bitcast(BF16)
            _cast_dma(v1[:, 0:CT * C], "wkt", buf("A_w1"), "wc")
            _cast_dma(v1[:, CT * C:2 * CT * C], "wvt", buf("A_w1b"), "wc")
            _cast_dma(v2[:, 0:CT * C], "wrt", buf("A_w2"), "wc")
            return v1, v2

        def cast_attn_weights_o():
            v2 = A_w2[:].bitcast(BF16)
            _cast_dma(v2[:, CT * C:2 * CT * C], "wot", buf("A_w2b"), "wc")

        def plan_ln(xa_flat, xa3, xbf, n, xb, xb_buf, dest_halves, dest_buf,
                    tmp_halves, tmp_buf, skip_cast=False):
            if not skip_cast:
                w_op("vector", lambda e: e.tensor_copy(out=xb, in_=xa_flat),
                     [xb_buf], [xbf])
            xsq = A_xsq[:, 0:CT * n]
            w_op("vector", lambda e, xsq=xsq: e.tensor_tensor(out=xsq, in0=xb, in1=xb,
                                                              op=ALU.mult),
                 [buf("A_xsq")], [xb_buf])
            def stats_group(psd, psb, srcbuf, src):
                waits = list(srcbuf.rdeps()) + list(buf("ones_bf").rdeps()) + \
                    list(srcbuf.writers) + psb.wdeps()
                tok = None
                for ct in range(CT):
                    fn = (lambda e, ct=ct: e.matmul(
                        psd[0:1, 0:n], ones_bf[:], src[:, ct * n:(ct + 1) * n],
                        start=(ct == 0), stop=(ct == CT - 1)))
                    if ct == 0:
                        P.op("tensor", fn, waits, inc=0)
                    elif ct == CT - 1:
                        tok = P.op("tensor", fn, (), inc=1)
                    else:
                        P.op("tensor", fn, (), inc=0)
                srcbuf.addr(tok)
                buf("ones_bf").addr(tok)
                psb.setw(tok)
            stats_group(ps_s1, buf("ps_s1"), xb_buf, xb)
            stats_group(ps_s2, buf("ps_s2"), buf("A_xsq"), xsq)

            mrow = WKB[0][0:1, 0:n]
            erow = WKB[1][0:1, 0:n]
            m2 = WKB[2][0:1, 0:n]
            w_op("vector", lambda e: e.tensor_scalar(out=mrow, in0=ps_s1[0:1, 0:n],
                                                     scalar1=float(1.0 / C),
                                                     scalar2=None, op0=ALU.mult),
                 [buf("wk0")], [buf("ps_s1")])
            w_op("vector", lambda e: e.tensor_scalar(out=erow, in0=ps_s2[0:1, 0:n],
                                                     scalar1=float(1.0 / C),
                                                     scalar2=None, op0=ALU.mult),
                 [buf("wk1")], [buf("ps_s2")])
            w_op("vector", lambda e: e.tensor_tensor(out=m2, in0=mrow, in1=mrow,
                                                     op=ALU.mult),
                 [buf("wk2")], [buf("wk0")])
            w_op("vector", lambda e: e.tensor_tensor(out=erow, in0=erow, in1=m2,
                                                     op=ALU.subtract),
                 [buf("wk1")], [buf("wk1"), buf("wk2")])
            w_op("vector", lambda e: e.tensor_scalar(out=erow, in0=erow,
                                                     scalar1=1e-5, scalar2=None,
                                                     op0=ALU.add),
                 [buf("wk1")], [buf("wk1")])
            w_op("scalar", lambda e: e.activation(out=erow, in_=erow, func=AF.Ln),
                 [buf("wk1")], [buf("wk1")])
            w_op("scalar", lambda e: e.activation(out=erow, in_=erow, func=AF.Exp,
                                                  scale=-0.5),
                 [buf("wk1")], [buf("wk1")])
            w_op("tensor", lambda e: e.matmul(ps_bc1[:, 0:n], ones_k1[:], mrow,
                                              start=True, stop=True),
                 [buf("ps_bc1")], [buf("wk0"), buf("ones_k1")])
            w_op("tensor", lambda e: e.matmul(ps_bc2[:, 0:n], ones_k1[:], erow,
                                              start=True, stop=True),
                 [buf("ps_bc2")], [buf("wk1"), buf("ones_k1")])
            h = CT // 2
            for g in range(2):
                xs = xa3[:, g * h:(g + 1) * h, :]
                bc1 = ps_bc1[:, 0:n].unsqueeze(1).broadcast_to([128, h, n])
                bc2 = ps_bc2[:, 0:n].unsqueeze(1).broadcast_to([128, h, n])
                w_op("vector", lambda e, xs=xs, g=g, bc1=bc1: e.tensor_tensor(
                        out=tmp_halves[g], in0=xs, in1=bc1, op=ALU.subtract),
                     [tmp_buf], [xbf, buf("ps_bc1")])
                w_op("vector", lambda e, g=g, bc2=bc2: e.tensor_tensor(
                        out=dest_halves[g], in0=tmp_halves[g], in1=bc2,
                        op=ALU.mult),
                     [dest_buf], [tmp_buf, buf("ps_bc2")])

        def plan_ffn(n, ybv, uq_flat, uqb):
            hq = A_hq[:]
            DR = mybir.MatmulPerfMode.DoubleRow
            dr = n >= 256
            w13 = A_w1[:].rearrange("p (c d) -> p c d", c=CT)
            uq3 = uq_flat[:, 0:CT * n].rearrange("p (c n) -> p c n", c=CT)
            for j in range(HT):
                psd = ps_mm[j % 2]
                psb = buf(f"ps_mm{j % 2}")
                waits = uqb.rdeps() + buf("A_w1").rdeps() + \
                    buf("A_w1b").rdeps() + psb.wdeps()
                tok = None
                if dr:
                    for t in range(CT // 2):
                        fn = (lambda e, t=t, j=j, psd=psd: e.matmul(
                            psd[:, 0:n],
                            w13[:, 2 * t:2 * t + 2, j * 128:(j + 1) * 128],
                            uq3[:, 2 * t:2 * t + 2, :],
                            start=(t == 0), stop=(t == CT // 2 - 1),
                            perf_mode=DR))
                        if t == 0:
                            P.op("tensor", fn, waits, inc=0)
                        elif t == CT // 2 - 1:
                            tok = P.op("tensor", fn, (), inc=1)
                        else:
                            P.op("tensor", fn, (), inc=0)
                else:
                    for ct in range(CT):
                        fn = (lambda e, ct=ct, j=j, psd=psd: e.matmul(
                            psd[:, 0:n],
                            A_w1[:, ct * DFF + j * 128: ct * DFF + (j + 1) * 128],
                            uq_flat[:, ct * n:(ct + 1) * n],
                            start=(ct == 0), stop=(ct == CT - 1)))
                        if ct == 0:
                            P.op("tensor", fn, waits, inc=0)
                        elif ct == CT - 1:
                            tok = P.op("tensor", fn, (), inc=1)
                        else:
                            P.op("tensor", fn, (), inc=0)
                uqb.addr(tok)
                buf("A_w1").addr(tok)
                buf("A_w1b").addr(tok)
                psb.setw(tok)
                w_op("scalar", lambda e, j=j, psd=psd: e.activation(
                        out=hq[:, j * n:(j + 1) * n], in_=psd[:, 0:n], func=AF.Relu,
                        bias=c_b1[:, j:j + 1], scale=1.0),
                     [buf("A_hq")], [psb, buf("b1c")])
            w23 = A_w2[:].rearrange("p (c d) -> p c d", c=HT)
            hq3 = hq[:, 0:HT * n].rearrange("p (c n) -> p c n", c=HT)
            for j2 in range(CT):
                psd = ps_mm[2 + j2 % 2]
                psb = buf(f"ps_mm{2 + j2 % 2}")
                waits = buf("A_hq").rdeps() + buf("A_w2").rdeps() + \
                    buf("A_w2b").rdeps() + psb.wdeps()
                tok = None
                if dr:
                    for t in range(HT // 2):
                        fn = (lambda e, t=t, j2=j2, psd=psd: e.matmul(
                            psd[:, 0:n],
                            w23[:, 2 * t:2 * t + 2, j2 * 128:(j2 + 1) * 128],
                            hq3[:, 2 * t:2 * t + 2, :],
                            start=(t == 0), stop=(t == HT // 2 - 1),
                            perf_mode=DR))
                        if t == 0:
                            P.op("tensor", fn, waits, inc=0)
                        elif t == HT // 2 - 1:
                            tok = P.op("tensor", fn, (), inc=1)
                        else:
                            P.op("tensor", fn, (), inc=0)
                else:
                    for ht in range(HT):
                        fn = (lambda e, ht=ht, j2=j2, psd=psd: e.matmul(
                            psd[:, 0:n],
                            A_w2[:, ht * C + j2 * 128: ht * C + (j2 + 1) * 128],
                            hq[:, ht * n:(ht + 1) * n],
                            start=(ht == 0), stop=(ht == HT - 1)))
                        if ht == 0:
                            P.op("tensor", fn, waits, inc=0)
                        elif ht == HT - 1:
                            tok = P.op("tensor", fn, (), inc=1)
                        else:
                            P.op("tensor", fn, (), inc=0)
                buf("A_hq").addr(tok)
                buf("A_w2").addr(tok)
                buf("A_w2b").addr(tok)
                psb.setw(tok)
                w_op("scalar", lambda e, j2=j2, psd=psd: e.activation(
                        out=ybv[:, j2 * n:(j2 + 1) * n], in_=psd[:, 0:n],
                        func=AF.Identity, bias=c_b2[:, j2:j2 + 1], scale=1.0),
                     [buf("A_yb")], [psb, buf("b2c")])

        def plan_ffn_phase(srcsp, dstsp, blocks, poff=0, dr_prefix=None,
                           src_prefix=None, part="all", alt0=False, pre_k=1,
                           delay1=False):
            blocks = list(blocks)

            def src_rb(b):
                return [buf(f"{src_prefix}{b}")] if src_prefix else []

            def loads(bi, b):
                s, n = BLOCKS[b]
                par = (bi + poff) % 2
                xa = A_xin[:, par * CT * 512: par * CT * 512 + CT * n]
                src_ = d_in[f"x0_{b}"] if srcsp is None else srcsp[b]
                w_op("sync", lambda e, xa=xa, src_=src_: e.dma_start(out=xa, in_=src_),
                     [buf(f"xin{par}")], src_rb(b), inc=16)

            def ln(bi, b):
                s, n = BLOCKS[b]
                par = (bi + poff) % 2
                xa = A_xin[:, par * CT * 512: par * CT * 512 + CT * n]
                xa3 = xa.rearrange("p (c n) -> p c n", c=CT)
                if alt0 and bi == 0:
                    xb = A_u2[:, 0:CT * n]
                    xb_b = buf("A_u2")
                else:
                    xb = A_srw[:, 0:CT * n]
                    xb_b = buf("A_srw")
                src_ = d_in[f"x0_{b}"] if srcsp is None else srcsp[b]
                w_op("gpsimd", lambda e, xb=xb, src_=src_: e.dma_start(
                        out=xb, in_=src_),
                     [xb_b], src_rb(b), inc=16)
                uqs = uq_slots[par][:, 0:CT * n]
                uq3 = uqs.rearrange("p (c n) -> p c n", c=CT)
                h = CT // 2
                tmp3 = ab_tmp[:, 0:h * n].rearrange("p (c n) -> p c n", c=h)
                plan_ln(xa, xa3, buf(f"xin{par}"), n, xb, xb_b,
                        [uq3[:, 0:h, :], uq3[:, h:CT, :]], buf(f"uq{par}"),
                        [tmp3, tmp3], buf("ab_tmp"), skip_cast=True)

            def body(bi, b):
                s, n = BLOCKS[b]
                par = (bi + poff) % 2
                xa = A_xin[:, par * CT * 512: par * CT * 512 + CT * n]
                w_op("sync", lambda e, n=n, b=b: e.dma_start(
                        out=A_msk[:, 0:CT * n], in_=d_in[f"mk_{b}"]),
                     [buf("A_msk")], [], inc=16)
                ybv = A_yb[:]
                w_op("vector", lambda e, xa=xa, n=n: e.tensor_copy(
                        out=stage_ab[:, 0:CT * n], in_=xa),
                     [buf("stage")], [buf(f"xin{par}")])
                plan_ffn(n, ybv, uq_slots[par][:, 0:CT * n], buf(f"uq{par}"))
                dv = A_d[:]
                w_op("vector", lambda e, n=n: e.tensor_tensor(
                        out=dv[:, 0:CT * n], in0=ybv[:, 0:CT * n],
                        in1=A_msk[:, 0:CT * n], op=ALU.mult),
                     [buf("A_d")], [buf("A_yb"), buf("A_msk")])
                w_op("vector", lambda e, n=n: e.tensor_tensor(
                        out=stage_ab[:, 0:CT * n], in0=stage_ab[:, 0:CT * n],
                        in1=dv[:, 0:CT * n], op=ALU.add),
                     [buf("stage")], [buf("stage"), buf("A_d")])
                dst = dstsp[b]
                w_op("scalar", lambda e, dst=dst, n=n: e.dma_start(
                        out=dst, in_=stage_ab[:, 0:CT * n]),
                     [buf(f"{dr_prefix}{b}")] if dr_prefix else [],
                     [buf("stage")], sname="sts", inc=16)

            if part in ("pre", "all"):
                for i in range(min(pre_k, len(blocks))):
                    loads(i, blocks[i])
                    ln(i, blocks[i])
            if part == "pre":
                return
            for i, b in enumerate(blocks):
                j = i + 1
                emit_lead = j < len(blocks) and j >= pre_k
                if emit_lead and not (delay1 and i == 0):
                    loads(j, blocks[j])
                    ln(j, blocks[j])
                body(i, b)
                if emit_lead and delay1 and i == 0:
                    loads(j, blocks[j])
                    ln(j, blocks[j])

        cde_prev_n = [None]

        def plan_cde(wv1, wv2, part="all", wo_cast=None):
            def a_load(b):
                s, n = BLOCKS[b]
                par = b % 2
                xa = A_xin[:, par * CT * 512: par * CT * 512 + CT * n]
                w_op("sync", lambda e, xa=xa, b=b: e.dma_start(out=xa, in_=x1sp[b]),
                     [buf(f"xin{par}")], [buf(f"x1d_{b}")], inc=16)

            def a_shadow(b):
                s, n = BLOCKS[b]
                xb = A_srw[:, 0:CT * n]
                w_op("gpsimd", lambda e, xb=xb, b=b: e.dma_start(
                        out=xb, in_=x1sp[b]),
                     [buf("A_srw")], [buf(f"x1d_{b}")], inc=16)

            if part == "pre":
                a_load(0)
                a_load(1)
                a_shadow(0)
                return
            WkT = wv1
            WvT_off = CT * C
            WrT = wv2
            WoT_off = CT * C
            srb = A_yb[:]
            srw = A_srw[:]
            kv16 = A_kv[:].bitcast(mybir.dt.float16)
            kvb = A_kv[:].bitcast(BF16)
            hqb = A_hq[:].bitcast(BF16)
            FP16 = mybir.dt.float16

            def kslot(b):
                p = b % 2
                return kv16[:, p * CT * 512:(p + 1) * CT * 512]

            def vslot(b):
                p = b % 2
                return kvb[:, 2 * CT * 512 + p * CT * 512:
                           2 * CT * 512 + (p + 1) * CT * 512]

            mixk = A_xsq[:]
            mixv_ = hqb[:, 0:CT * 512]
            mixr = hqb[:, CT * 512:2 * CT * 512]
            mix_of = {"k": (mixk, buf("A_xsq")), "v": (mixv_, buf("mixv")),
                      "r": (mixr, buf("mixr"))}


            nb = len(BLOCKS)
            prev_n = cde_prev_n

            def stageA(b):
                s, n = BLOCKS[b]
                par = b % 2
                xa = A_xin[:, par * CT * 512: par * CT * 512 + CT * n]
                xbf = buf(f"xin{par}")
                if b >= 2:
                    a_load(b)
                if prev_n[0] is not None:
                    w_op("vector", lambda e, pn=prev_n[0]: e.tensor_copy(
                            out=u2v[:, :, 0:1], in_=u2v[:, :, pn:pn + 1]),
                         [buf("A_u2")], [buf("A_u2")])
                prev_n[0] = n
                xb = A_srw[:, 0:CT * n]
                xa3 = xa.rearrange("p (c n) -> p c n", c=CT)
                h = CT // 2
                ctmp = A_wk[:, 0:h * 512][:, 0:h * n] if False else \
                    A_wk[:, 0:2048][:, 0:h * n].rearrange("p (c n) -> p c n", c=h)
                wkbufs = [buf("wk0"), buf("wk1"), buf("wk2"), buf("wk3")]
                if b >= 1:
                    a_shadow(b)
                xsq = mixk[:, 0:CT * n]
                w_op("scalar", lambda e, xsq=xsq, xb=xb: e.activation(
                        out=xsq, in_=xb, func=AF.Square),
                     [buf("A_xsq")], [buf("A_srw")])
                def stats_group(psd, psb, srcbuf, src):
                    waits = list(srcbuf.rdeps()) + list(buf("ones_bf").rdeps()) + \
                        list(srcbuf.writers) + psb.wdeps()
                    tok = None
                    for ct in range(CT):
                        fn = (lambda e, ct=ct: e.matmul(
                            psd[0:1, 0:n], ones_bf[:], src[:, ct * n:(ct + 1) * n],
                            start=(ct == 0), stop=(ct == CT - 1)))
                        if ct == 0:
                            P.op("tensor", fn, waits, inc=0)
                        elif ct == CT - 1:
                            tok = P.op("tensor", fn, (), inc=1)
                        else:
                            P.op("tensor", fn, (), inc=0)
                    srcbuf.addr(tok)
                    buf("ones_bf").addr(tok)
                    psb.setw(tok)
                stats_group(ps_s1, buf("ps_s1"), buf("A_srw"), xb)
                stats_group(ps_s2, buf("ps_s2"), buf("A_xsq"), xsq)
                mrow = WKB[0][0:1, 0:n]
                erow = WKB[1][0:1, 0:n]
                m2 = WKB[2][0:1, 0:n]
                w_op("vector", lambda e, mrow=mrow: e.tensor_scalar(
                        out=mrow, in0=ps_s1[0:1, 0:n], scalar1=float(1.0 / C),
                        scalar2=None, op0=ALU.mult),
                     [buf("wk0")], [buf("ps_s1")])
                w_op("vector", lambda e, erow=erow: e.tensor_scalar(
                        out=erow, in0=ps_s2[0:1, 0:n], scalar1=float(1.0 / C),
                        scalar2=None, op0=ALU.mult),
                     [buf("wk1")], [buf("ps_s2")])
                w_op("vector", lambda e, m2=m2, mrow=mrow: e.tensor_tensor(
                        out=m2, in0=mrow, in1=mrow, op=ALU.mult),
                     [buf("wk2")], [buf("wk0")])
                w_op("vector", lambda e, erow=erow, m2=m2: e.tensor_tensor(
                        out=erow, in0=erow, in1=m2, op=ALU.subtract),
                     [buf("wk1")], [buf("wk1"), buf("wk2")])
                w_op("vector", lambda e, erow=erow: e.tensor_scalar(
                        out=erow, in0=erow, scalar1=1e-5, scalar2=None, op0=ALU.add),
                     [buf("wk1")], [buf("wk1")])
                w_op("scalar", lambda e, erow=erow: e.activation(
                        out=erow, in_=erow, func=AF.Ln),
                     [buf("wk1")], [buf("wk1")])
                w_op("scalar", lambda e, erow=erow: e.activation(
                        out=erow, in_=erow, func=AF.Exp, scale=-0.5),
                     [buf("wk1")], [buf("wk1")])
                w_op("tensor", lambda e, mrow=mrow: e.matmul(
                        ps_bc1[:, 0:n], ones_k1[:], mrow, start=True, stop=True),
                     [buf("ps_bc1")], [buf("wk0"), buf("ones_k1")])
                w_op("tensor", lambda e, erow=erow: e.matmul(
                        ps_bc2[:, 0:n], ones_k1[:], erow, start=True, stop=True),
                     [buf("ps_bc2")], [buf("wk1"), buf("ones_k1")])
                for g in range(2):
                    xs = xa3[:, g * h:(g + 1) * h, :]
                    dst = u2v[:, g * h:(g + 1) * h, 1:n + 1]
                    bc1 = ps_bc1[:, 0:n].unsqueeze(1).broadcast_to([128, h, n])
                    bc2 = ps_bc2[:, 0:n].unsqueeze(1).broadcast_to([128, h, n])
                    w_op("vector", lambda e, xs=xs, bc1=bc1, ctmp=ctmp: e.tensor_tensor(
                            out=ctmp, in0=xs, in1=bc1, op=ALU.subtract),
                         wkbufs, [xbf, buf("ps_bc1")])
                    w_op("vector", lambda e, dst=dst, bc2=bc2, ctmp=ctmp: e.tensor_tensor(
                            out=dst, in0=ctmp, in1=bc2, op=ALU.mult),
                         [buf("A_u2")], wkbufs + [buf("ps_bc2")])
                dvv = A_d[:].rearrange("p (c n) -> p c n", c=CT)
                w_op("vector", lambda e, n=n, dvv=dvv: e.tensor_tensor(
                        out=dvv[:, :, 0:n], in0=u2v[:, :, 1:n + 1],
                        in1=u2v[:, :, 0:n], op=ALU.subtract),
                     [buf("A_d")], [buf("A_u2")])
                for pnm, tmc, tmb in (("k", c_tmk, buf("tmkc")),
                                      ("v", c_tmv, buf("tmvc")),
                                      ("r", c_tmr, buf("tmrc"))):
                    mv, mb = mix_of[pnm]
                    m3 = mv[:, 0:CT * n].rearrange("p (c n) -> p c n", c=CT)
                    for ct in range(CT):
                        w_op("scalar", lambda e, ct=ct, tmc=tmc, n=n, mv=mv, dvv=dvv:
                             e.activation(out=mv[:, ct * n:(ct + 1) * n],
                                          in_=dvv[:, ct, 0:n], func=AF.Identity,
                                          scale=tmc[:, ct:ct + 1]),
                             [mb], [buf("A_d"), tmb])
                    w_op("vector", lambda e, m3=m3, n=n: e.tensor_tensor(
                            out=m3, in0=m3, in1=u2v[:, :, 0:n], op=ALU.add),
                         [mb], [mb, buf("A_u2")])

            def _proj_mms(pnm, j, b, n, psd, psb):
                WT, woff, wtb = {"k": (WkT, 0, buf("A_w1")),
                                 "v": (WkT, WvT_off, buf("A_w1b")),
                                 "r": (WrT, 0, buf("A_w2"))}[pnm]
                mv, mb = mix_of[pnm]
                waits = mb.rdeps() + wtb.rdeps() + psb.wdeps()
                tok = None
                for ct in range(CT):
                    fn = (lambda e, ct=ct, j=j, psd=psd, WT=WT, woff=woff,
                          mv=mv, n=n: e.matmul(
                        psd[:, 0:n],
                        WT[:, woff + ct * C + j * 128:
                           woff + ct * C + (j + 1) * 128],
                        mv[:, ct * n:(ct + 1) * n],
                        start=(ct == 0), stop=(ct == CT - 1)))
                    if ct == 0:
                        P.op("tensor", fn, waits, inc=0)
                    elif ct == CT - 1:
                        tok = P.op("tensor", fn, (), inc=1)
                    else:
                        P.op("tensor", fn, (), inc=0)
                mb.addr(tok)
                wtb.addr(tok)
                psb.setw(tok)

            def stageB_kv_tiles(b, js):
                s, n = BLOCKS[b]
                ks = kslot(b)
                vs = vslot(b)
                p = b % 2
                for j in js:
                    for pnm in ("k", "v"):
                            psd = ps_mm[0 if pnm == "k" else 1]
                            psb = buf("ps_mm0" if pnm == "k" else "ps_mm1")
                            _proj_mms(pnm, j, b, n, psd, psb)
                            if pnm == "k":
                                if b == 0:
                                    fn2 = (lambda e, j=j, psd=psd, n=n, ks=ks:
                                           e.activation(
                                        out=ks[:, j * 512:j * 512 + n],
                                        in_=psd[:, 0:n], func=AF.Identity,
                                        bias=c_kb[:, j:j + 1], scale=1.0))
                                    rb = [psb, buf("kbc")]
                                else:
                                    fn2 = (lambda e, j=j, psd=psd, n=n, ks=ks:
                                           e.activation(
                                        out=ks[:, j * 512:j * 512 + n],
                                        in_=psd[:, 0:n], func=AF.Identity,
                                        scale=1.0))
                                    rb = [psb]
                            else:
                                fn2 = (lambda e, j=j, psd=psd, n=n, vs=vs:
                                       e.activation(
                                    out=vs[:, j * 512:j * 512 + n],
                                    in_=psd[:, 0:n], func=AF.Identity,
                                    scale=1.0))
                                rb = [psb]
                            w_op("scalar", fn2, [buf(f"{pnm}par{p}j{j}")], rb)

            def stageB(b, which=("k", "v", "r")):
                s, n = BLOCKS[b]
                if "k" in which or "v" in which:
                    stageB_kv_tiles(b, range(CT))
                if "r" in which:
                    for j in range(CT):
                        psd = ps_mm[j % 2]
                        psb = buf(f"ps_mm{j % 2}")
                        _proj_mms("r", j, b, n, psd, psb)
                        fn2 = (lambda e, j=j, psd=psd, n=n: e.activation(
                            out=srb[:, j * 512:j * 512 + n], in_=psd[:, 0:n],
                            func=AF.Sigmoid, scale=1.0))
                        w_op("scalar", fn2, [buf("dst_r")], [psb])

            def stageC(b, interleave=None):
                s, n = BLOCKS[b]
                ks = kslot(b)
                vs = vslot(b)
                pp = b % 2
                rb_ = buf("dst_r")
                dW = A_d[:].bitcast(F32)
                sets = []
                for si in range(2):
                    if si == 0:
                        wk = WKB
                        wkb = [buf(f"wk{i}") for i in range(4)]
                        pab = mskf[:, 0:1539]
                        pabb = buf("pabA")
                    else:
                        wk = [dW[:, i * 512:(i + 1) * 512] for i in range(4)]
                        wkb = [buf(f"wkB{i}") for i in range(4)]
                        pab = A_pab2[:]
                        pabb = buf("pabB")
                    sets.append((wk, wkb, pab, pabb))

                def chain_steps(ct):
                    wk, wkb, pab, pabb = sets[ct % 2]
                    kb_ = buf(f"kpar{pp}j{ct}")
                    vb_ = buf(f"vpar{pp}j{ct}")
                    Ap = pab[:, 0:513]
                    Aa = pab[:, 513:1026]
                    Ab = pab[:, 1026:1539]
                    kap = ks[:, ct * 512:ct * 512 + n]
                    vap = vs[:, ct * 512:ct * 512 + n]
                    st3o = pab[:, 0:1539:513]
                    st3i = st_all[:, ct:ct + 2 * CT + 1:CT]
                    pe3o = pab[:, n:n + 1027:513]
                    steps = []
                    steps.append(("vector", lambda e: e.tensor_copy(out=st3o, in_=st3i),
                                  [pabb], [buf("st_all")]))
                    steps.append(("vector", lambda e: e.tensor_tensor_scan(
                            out=Ap[:, 1:n + 1],
                            data0=c_w[:, ct:ct + 1].broadcast_to([128, n]),
                            data1=kap, initial=st_all[:, ct:ct + 1],
                            op0=ALU.add, op1=ALU.max),
                        [pabb], [kb_, buf("c_w"), buf("st_all")]))
                    steps.append(("vector", lambda e: e.tensor_tensor(
                            out=wk[0][:, 0:n], in0=Ap[:, 0:n], in1=Ap[:, 1:n + 1],
                            op=ALU.subtract),
                        [wkb[0]], [pabb]))
                    steps.append(("vector", lambda e: e.tensor_tensor(
                            out=wk[2][:, 0:n], in0=kap, in1=Ap[:, 1:n + 1],
                            op=ALU.subtract),
                        [wkb[2]], [kb_, pabb]))
                    steps.append(("scalar", lambda e: e.activation(
                            out=wk[1][:, 0:n], in_=wk[0][:, 0:n], func=AF.Exp,
                            bias=c_w[:, ct:ct + 1], scale=1.0),
                        [wkb[1]], [wkb[0], buf("c_w")]))
                    steps.append(("scalar", lambda e: e.activation(
                            out=wk[3][:, 0:n], in_=wk[2][:, 0:n], func=AF.Exp),
                        [wkb[3]], [wkb[2]]))
                    steps.append(("vector", lambda e: e.tensor_tensor(
                            out=wk[0][:, 0:n], in0=wk[3][:, 0:n], in1=vap,
                            op=ALU.mult),
                        [wkb[0]], [wkb[3], vb_]))
                    steps.append(("vector", lambda e: e.tensor_tensor_scan(
                            out=Aa[:, 1:n + 1], data0=wk[1][:, 0:n],
                            data1=wk[0][:, 0:n], initial=st_all[:, CT + ct:CT + ct + 1],
                            op0=ALU.mult, op1=ALU.add),
                        [pabb], [wkb[1], wkb[0], buf("st_all")]))
                    steps.append(("vector", lambda e: e.tensor_tensor_scan(
                            out=Ab[:, 1:n + 1], data0=wk[1][:, 0:n],
                            data1=wk[3][:, 0:n], initial=st_all[:, 2 * CT + ct:2 * CT + ct + 1],
                            op0=ALU.mult, op1=ALU.add),
                        [pabb], [wkb[1], wkb[3], buf("st_all")]))
                    steps.append(("vector", lambda e: e.tensor_copy(out=st3i, in_=pe3o),
                                  [buf("st_all")], [pabb]))
                    if b == 0:
                        return steps
                    steps.append(("vector", lambda e: e.tensor_tensor(
                            out=wk[2][:, 0:n], in0=kap, in1=Ap[:, 0:n],
                            op=ALU.subtract),
                        [wkb[2]], [kb_, pabb]))
                    steps.append(("vector", lambda e: e.tensor_scalar(
                            out=wk[0][:, 0:n], in0=wk[2][:, 0:n], scalar1=20.0,
                            scalar2=None, op0=ALU.min),
                        [wkb[0]], [wkb[2]]))
                    steps.append(("scalar", lambda e: e.activation(
                            out=wk[2][:, 0:n], in_=wk[0][:, 0:n], func=AF.Exp,
                            bias=c_u[:, ct:ct + 1], scale=1.0),
                        [wkb[2]], [wkb[0], buf("c_u")]))
                    steps.append(("vector", lambda e: e.tensor_tensor(
                            out=wk[0][:, 0:n], in0=wk[2][:, 0:n], in1=vap,
                            op=ALU.mult),
                        [wkb[0]], [wkb[2], vb_]))
                    steps.append(("vector", lambda e: e.tensor_tensor(
                            out=wk[3][:, 0:n], in0=Aa[:, 0:n], in1=wk[0][:, 0:n],
                            op=ALU.add),
                        [wkb[3]], [pabb, wkb[0]]))
                    steps.append(("vector", lambda e: e.tensor_tensor(
                            out=wk[0][:, 0:n], in0=Ab[:, 0:n], in1=wk[2][:, 0:n],
                            op=ALU.add),
                        [wkb[0]], [pabb, wkb[2]]))
                    steps.append(("scalar", lambda e: e.activation(
                            out=wk[2][:, 0:n], in_=wk[0][:, 0:n], func=AF.Ln),
                        [wkb[2]], [wkb[0]]))
                    steps.append(("scalar", lambda e: e.activation(
                            out=wk[0][:, 0:n], in_=wk[2][:, 0:n], func=AF.Exp,
                            scale=-1.0),
                        [wkb[0]], [wkb[2]]))
                    steps.append(("vector", lambda e: e.tensor_tensor(
                            out=wk[2][:, 0:n], in0=wk[3][:, 0:n], in1=wk[0][:, 0:n],
                            op=ALU.mult),
                        [wkb[2]], [wkb[3], wkb[0]]))
                    steps.append(("vector", lambda e, ct=ct: e.tensor_tensor(
                            out=srw[:, ct * 512:ct * 512 + n], in0=wk[2][:, 0:n],
                            in1=srb[:, ct * 512:ct * 512 + n], op=ALU.mult),
                        [buf("A_srw"), buf(f"srwj{ct}")], [wkb[2], rb_]))
                    return steps

                for c0 in range(0, CT, 2):
                    if interleave is not None:
                        interleave(c0 // 2)
                    s0 = chain_steps(c0)
                    s1 = chain_steps(c0 + 1)
                    for st0, st1 in zip(s0, s1):
                        for eng, fn, wb, rb in (st0, st1):
                            w_op(eng, fn, wb, rb)
                tokA = ("v", P.cnt["v"])
                buf("A_d").addr(tokA)
                tokS = ("s", P.cnt["s"])
                buf("A_d").addr(tokS)

            def stageD(b):
                if b == 0:
                    return
                s, n = BLOCKS[b]
                par = b % 2
                xa = A_xin[:, par * CT * 512: par * CT * 512 + CT * n]
                xbf = buf(f"xin{par}")
                srwb = buf("A_srw")
                for j in range(CT):
                    psd = ps_mm[2 + j % 2]
                    psb = buf(f"ps_mm{2 + j % 2}")
                    waits = buf("A_w2b").rdeps() + psb.wdeps()
                    tok = None
                    for ct in range(CT):
                        fn = (lambda e, ct=ct, j=j, psd=psd, n=n: e.matmul(
                            psd[:, 0:n],
                            wv2[:, WoT_off + ct * C + j * 128:
                                WoT_off + ct * C + (j + 1) * 128],
                            srw[:, ct * 512:ct * 512 + n],
                            start=(ct == 0), stop=(ct == CT - 1)))
                        w = list(buf(f"srwj{ct}").rdeps())
                        if ct == 0:
                            P.op("tensor", fn, waits + w, inc=0)
                        elif ct == CT - 1:
                            tok = P.op("tensor", fn, w, inc=1)
                        else:
                            P.op("tensor", fn, w, inc=0)
                    srwb.addr(tok)
                    buf("A_w2b").addr(tok)
                    psb.setw(tok)
                    w_op("vector", lambda e, j=j, psd=psd, xa=xa, n=n: e.tensor_tensor(
                            out=xa[:, j * n:(j + 1) * n], in0=psd[:, 0:n],
                            in1=xa[:, j * n:(j + 1) * n], op=ALU.add),
                         [xbf], [psb, xbf])
                w_op("scalar", lambda e, b=b, n=n, xa=xa: e.dma_start(
                        out=x2sp[b], in_=xa[:, 0:CT * n]),
                     [buf(f"x2d_{b}")], [xbf], sname="sts", inc=16)

            nbk = len(BLOCKS)
            if part == "pre2":
                link(buf("mixv"), buf("A_hq"))
                link(buf("mixr"), buf("A_hq"))
                link(buf("dst_r"), buf("A_yb"))
                for j in range(CT):
                    link(buf(f"kpar0j{j}"), buf("stage"))
                    link(buf(f"vpar0j{j}"), buf("stage"))
                    link(buf(f"vpar0j{j}"), buf("uq0"))
                    link(buf(f"vpar0j{j}"), buf("uq1"))
                w_op("vector", lambda e: e.memset(u2v[:, :, 0:1], 0.0),
                     [buf("A_u2")], [])
                stageA(0)
                stageB(0)
                stageA(1)
                return
            if wo_cast is not None:
                wo_cast()
            for b in range(1, nbk):
                stageC(b - 1,
                       interleave=lambda jp, b=b: stageB_kv_tiles(
                           b, (2 * jp, 2 * jp + 1)))
                stageD(b - 1)
                stageB(b, which=("r",))
                if b + 1 < nbk:
                    stageA(b + 1)
            stageC(nbk - 1)
            stageD(nbk - 1)

        def barrier(snap=None):
            toks = snap if snap is not None else \
                [(sn, P.cnt[sn]) for sn in
                 ("lds0", "lds1", "sts0", "sts1", "ldg0", "ldg1", "t", "v", "s")]
            for e in ENGS:
                for sn, v in toks:
                    if v > 0 and P.sem_owner[sn] != e:
                        P.raw_wait(e, sn, v)

        def snap_counts():
            return [(sn, P.cnt[sn]) for sn in
                    ("lds0", "lds1", "sts0", "sts1", "ldg0", "ldg1",
                     "t", "v", "s")]

        prologue_memsets()
        plan_ffn_phase(None, x1sp, range(len(BLOCKS)), dr_prefix="x1d_",
                       part="pre", pre_k=2)
        prologue_consts()
        cast_ffn_weights()
        plan_ffn_phase(None, x1sp, range(len(BLOCKS)), dr_prefix="x1d_",
                       part="rest", pre_k=2)
        _snap0 = snap_counts()
        plan_cde(None, None, part="pre")
        wv1, wv2 = cast_attn_weights_kvr()
        plan_cde(wv1, wv2, part="pre2")
        barrier(_snap0)
        plan_cde(wv1, wv2, part="rest", wo_cast=cast_attn_weights_o)
        _snap = snap_counts()
        for _j in range(CT):
            link(buf("uq1"), buf(f"vpar0j{_j}"))
            link(buf("ab_tmp"), buf(f"vpar1j{_j}"))
        plan_ffn_phase(x2sp, outs, range(1, len(BLOCKS)), poff=1,
                       dr_prefix="outd_", src_prefix="x2d_", part="pre",
                       alt0=True)
        cast_ffn_weights()
        barrier(_snap)
        plan_ffn_phase(x2sp, outs, range(1, len(BLOCKS)), poff=1,
                       dr_prefix="outd_", src_prefix="x2d_", part="rest",
                       alt0=True, delay1=True)
        P.raw_wait("scalar", "sts0", P.cnt["sts0"])
        P.raw_wait("scalar", "sts1", P.cnt["sts1"])
        P.raw_wait("gpsimd", "sts0", P.cnt["sts0"])
        P.raw_wait("gpsimd", "sts1", P.cnt["sts1"])

        with nc.Block() as block:
            @block.sync
            def _(e):
                for f in P.q["sync"]:
                    f(e)

            @block.scalar
            def _(e):
                for f in P.q["scalar"]:
                    f(e)

            @block.vector
            def _(e):
                for f in P.q["vector"]:
                    f(e)

            @block.tensor
            def _(e):
                for f in P.q["tensor"]:
                    f(e)

            @block.gpsimd
            def _(e):
                for f in P.q["gpsimd"]:
                    f(e)

    return nc


_NC_CACHE = {}


def kernel(L0_attn_x, L0_attn_time_decay, L0_attn_time_first, L0_attn_time_mix_k,
           L0_attn_time_mix_v, L0_attn_time_mix_r, L0_attn_Wk, L0_attn_Wv,
           L0_attn_Wr, L0_attn_Wo, L0_ffn_W1, L0_ffn_b1, L0_ffn_W2, L0_ffn_b2):
    x = np.asarray(L0_attn_x, dtype=np.float32).reshape(16384, 1024)
    xT = np.ascontiguousarray(x.T)
    maskT = np.ascontiguousarray(_mk_mask625().reshape(16384, 1024).T)

    w1t = _prep_weightT(L0_ffn_W1)
    w2t = _prep_weightT(L0_ffn_W2)
    wkt = _prep_weightT(L0_attn_Wk)
    wvt = _prep_weightT(L0_attn_Wv)
    wrt = _prep_weightT(L0_attn_Wr)
    wot = _prep_weightT(L0_attn_Wo)
    b1c = _colN(L0_ffn_b1, HT)
    b2c = _colN(L0_ffn_b2, CT)
    tdc = _colN(np.asarray(L0_attn_time_decay).reshape(-1), CT)
    tfc = _colN(np.asarray(L0_attn_time_first).reshape(-1), CT)
    tmkc = _colN(np.asarray(L0_attn_time_mix_k).reshape(-1), CT)
    tmvc = _colN(np.asarray(L0_attn_time_mix_v).reshape(-1), CT)
    tmrc = _colN(np.asarray(L0_attn_time_mix_r).reshape(-1), CT)

    if "nc" not in _NC_CACHE:
        _NC_CACHE["nc"] = build_program()
    nc = _NC_CACHE["nc"]

    in_maps = []
    for core in range(NCORES):
        S0 = core * TLOC
        xs = np.zeros((1024, TTOT), np.float32)
        ms = np.zeros((1024, TTOT), np.float32)
        lo = S0 - HALO
        src_lo = max(lo, 0)
        off = src_lo - lo
        xs[:, off:] = xT[:, src_lo:S0 + TLOC]
        ms[:, off:] = maskT[:, src_lo:S0 + TLOC]
        kb = np.zeros((128, CT), np.float32)
        if core == 0:
            kb[:] = -60000.0
        cst = np.ascontiguousarray(np.concatenate(
            [b1c, b2c, tdc, tfc, tmkc, tmvc, tmrc, kb], axis=1))
        im = {"w1t": w1t, "w2t": w2t, "wkt": wkt, "wvt": wvt, "wrt": wrt,
              "wot": wot, "cst": cst}
        for b, (s, n) in enumerate(BLOCKS):
            im[f"x0_{b}"] = _tile_cols(xs, s, n, np.float32)
            im[f"mk_{b}"] = _tile_cols(ms, s, n, np.float32).astype(ml_dtypes.bfloat16)
        in_maps.append(im)

    res = run_bass_kernel_spmd(nc, in_maps, core_ids=list(range(NCORES)))

    out = np.empty((16384, 1024), np.float32)
    for core in range(NCORES):
        S0 = core * TLOC
        r = res.results[core]
        for b, (s, n) in enumerate(BLOCKS):
            if b == 0:
                continue
            blk = np.asarray(r[f"x3_{b}"]).reshape(128, CT, n).transpose(1, 0, 2)
            g = S0 - HALO + s
            out[g:g + n, :] = blk.reshape(1024, n).T
    return out.reshape(1, 16384, 1024)


if __name__ == "__main__":
    nc = build_program()
    n_inst = sum(len(v) for v in nc.to_json().get("functions", [{}])[0].get(
        "instructions", [])) if False else "?"
    print("build ok")
